# version 11
# baseline (speedup 1.0000x reference)
import contextlib
import numpy as np

import concourse.bass as bass
import concourse.mybir as mybir
from concourse.bass_utils import run_bass_kernel_spmd
from concourse.bass_core_rust import IndirectOffsetOnAxis

F32 = mybir.dt.float32
BF16 = mybir.dt.bfloat16
AF = mybir.ActivationFunctionType
ALU = mybir.AluOpType
AX = mybir.AxisListType

D = 2048
NDC = 16
NH = 16
TOK = 2048
SEQ = 4096
NKC = 32
NQB = 16
E = 8
FF = 1408
NIC = 11
CAP = 1024
NCC = CAP // 128
SCALE = 1.0 / float(np.sqrt(128.0))
EPS = 1e-5


class Sched:

    def __init__(self):
        self.ops = []
        self.counts = {}

    def sem(self, name):
        self.counts[name] = 0
        return name

    def op(self, engine, fn, waits=(), incs=()):
        ws = []
        for w in waits:
            if w is None:
                continue
            s, v = w
            if v > 0:
                ws.append((s, v))
        self.ops.append((engine, fn, ws, list(incs)))
        out = []
        for s, n in incs:
            self.counts[s] += n
            out.append((s, self.counts[s]))
        return out

    def tok(self, name):
        return (name, self.counts[name])


def build_nc(nh=NH, ne=E, p1=True, p2=True, p3=True, p4=True, debug=False):
    nc = bass.Bass("TRN2", target_bir_lowering=False)
    spill_kind = "ExternalOutput" if debug else "Internal"

    x_own = nc.dram_tensor("x_own", [TOK, D], F32, kind="ExternalInput").ap()
    x_part = nc.dram_tensor("x_part", [TOK, D], F32, kind="ExternalInput").ap()
    wqT = nc.dram_tensor("wqT", [D, D], F32, kind="ExternalInput").ap()
    wkT = nc.dram_tensor("wkT", [D, D], F32, kind="ExternalInput").ap()
    wvT = nc.dram_tensor("wvT", [D, D], F32, kind="ExternalInput").ap()
    woT = nc.dram_tensor("woT", [D, D], F32, kind="ExternalInput").ap()
    bq_h = nc.dram_tensor("bq", [D], F32, kind="ExternalInput").ap()
    bk_h = nc.dram_tensor("bk", [D], F32, kind="ExternalInput").ap()
    bv_h = nc.dram_tensor("bv", [D], F32, kind="ExternalInput").ap()
    bo_h = nc.dram_tensor("bo", [D], F32, kind="ExternalInput").ap()
    wr_h = nc.dram_tensor("wr", [D, E], F32, kind="ExternalInput").ap()
    wg_h = nc.dram_tensor("wg", [E, D, FF], F32, kind="ExternalInput").ap()
    wu_h = nc.dram_tensor("wu", [E, D, FF], F32, kind="ExternalInput").ap()
    wd_h = nc.dram_tensor("wd", [E, FF, D], F32, kind="ExternalInput").ap()
    out_h = nc.dram_tensor("out", [TOK, D], F32, kind="ExternalOutput").ap()

    attn_hbm = nc.dram_tensor("attn_spill", [NH, 128, TOK], BF16,
                              kind=spill_kind).ap()
    xn_hbm = nc.dram_tensor("xn_spill", [TOK, D], BF16, kind=spill_kind).ap()
    wgb_hbm = nc.dram_tensor("wg_bf", [E, D, FF], BF16, kind="Internal").ap()
    wub_hbm = nc.dram_tensor("wu_bf", [E, D, FF], BF16, kind="Internal").ap()
    wdb_hbm = nc.dram_tensor("wd_bf", [E, FF, D], BF16, kind="Internal").ap()
    wob_hbm = nc.dram_tensor("wo_bf", [D, D], BF16, kind="Internal").ap()
    p_hbm = nc.dram_tensor("p_spill", [NQB, 128, E * 64], BF16,
                           kind=spill_kind).ap()
    pt_hbm = nc.dram_tensor("pt_spill", [NQB, 64, E * 128], BF16,
                            kind=spill_kind).ap()
    ycmp_hbm = [nc.dram_tensor("ycmp%d" % e, [CAP, D], BF16,
                               kind=spill_kind).ap() for e in range(E)]
    if debug:
        dbg_xng = nc.dram_tensor("dbg_xng", [128, NDC * CAP], BF16,
                                 kind="ExternalOutput").ap()
        dbg_h2 = nc.dram_tensor("dbg_h2", [128, NIC * CAP], BF16,
                                kind="ExternalOutput").ap()
        dbg_cb = nc.dram_tensor("dbg_cb", [128, NQB * E], F32,
                                kind="ExternalOutput").ap()
        dbg_h = nc.dram_tensor("dbg_h", [128, NIC * TOK], BF16,
                               kind="ExternalOutput").ap()
        dbg_r = nc.dram_tensor("dbg_r", [128, 32], F32,
                               kind="ExternalOutput").ap()
        dbg_lg = nc.dram_tensor("dbg_lg", [128, NQB * E], F32,
                                kind="ExternalOutput").ap()

    sc = Sched()
    for _sname in ["ldb", "lds", "lx0", "lx1", "swq", "swk", "swv",
                    "sa0", "sa1", "g0", "g1", "u0", "u1", "wdl",
                    "wo_", "at_", "yt", "xt4", "so", "dbg",
                    "lx2", "wcast",
                    "xns",
                    "psp",
                    "xg0", "xg1",
                    "pl0", "pl1",
                    "pts",
                    "cbs",
                    "ysp",
                    "ptl",
                    "ycl"]:
        sc.sem(_sname)
    PE = sc.sem("pe")
    VE = sc.sem("ve")
    SE = sc.sem("se")
    GO = sc.sem("go")

    es = contextlib.ExitStack()
    with es:
        ident_bf = es.enter_context(nc.sbuf_tensor("ident_bf", [128, 128], BF16))
        ident_f32 = es.enter_context(nc.sbuf_tensor("ident_f32", [128, 128], F32))
        ones_bf = es.enter_context(nc.sbuf_tensor("ones_bf", [128, 128], BF16))
        ones_col = es.enter_context(nc.sbuf_tensor("ones_col", [1, 128], BF16))
        iota_col = es.enter_context(nc.sbuf_tensor("iota_col", [128, 128], F32))
        iota_row = es.enter_context(nc.sbuf_tensor("iota_row", [128, 1], F32))
        eps_sb = es.enter_context(nc.sbuf_tensor("eps_sb", [128, 1], F32))
        bq_sb = es.enter_context(nc.sbuf_tensor("bq_sb", [128, 16], F32))
        bk_sb = es.enter_context(nc.sbuf_tensor("bk_sb", [128, 16], F32))
        bo_bc = es.enter_context(nc.sbuf_tensor("bo_bc", [128, D], F32))
        bv_bc = es.enter_context(nc.sbuf_tensor("bv_bc", [128, D], BF16))
        wr_sb = es.enter_context(nc.sbuf_tensor("wr_sb", [128, NDC * E], BF16))
        tri_bf = es.enter_context(nc.sbuf_tensor("tri_bf", [128, 128], BF16))
        cb_bf = es.enter_context(nc.sbuf_tensor("cb_bf", [128, NQB * E], BF16))

        sc.op("gpsimd", lambda g: g.iota(iota_col[:], [[1, 128]], base=0,
                                         channel_multiplier=0,
                                         allow_small_or_imprecise_dtypes=True),
              incs=[(GO, 1)])
        sc.op("gpsimd", lambda g: g.iota(iota_row[:], [[0, 1]], base=0,
                                         channel_multiplier=1,
                                         allow_small_or_imprecise_dtypes=True),
              incs=[(GO, 1)])
        t_iota = sc.tok(GO)
        sc.op("vector", lambda v: v.tensor_scalar(
            ident_bf[:], iota_col[:], iota_row[:], None, ALU.is_equal),
            waits=[t_iota], incs=[(VE, 1)])
        sc.op("vector", lambda v: v.tensor_scalar(
            ident_f32[:], iota_col[:], iota_row[:], None, ALU.is_equal),
            incs=[(VE, 1)])
        sc.op("vector", lambda v: v.memset(ones_bf[:], 1.0), incs=[(VE, 1)])
        sc.op("vector", lambda v: v.memset(ones_col[:], 1.0), incs=[(VE, 1)])
        sc.op("vector", lambda v: v.memset(eps_sb[:], EPS), incs=[(VE, 1)])
        sc.op("vector", lambda v: v.tensor_scalar(
            tri_bf[:], iota_col[:], iota_row[:], None, ALU.is_gt),
            incs=[(VE, 1)])
        t_setup_ve = sc.tok(VE)

        with contextlib.ExitStack() as es0:
            b16 = es0.enter_context(nc.sbuf_tensor("b16", [16, 128], F32))
            b16b = es0.enter_context(nc.sbuf_tensor("b16b", [16, 128], F32))
            brow = es0.enter_context(nc.sbuf_tensor("brow", [1, D], BF16))
            brow2 = es0.enter_context(nc.sbuf_tensor("brow2", [1, D], BF16))
            bps = es0.enter_context(nc.psum_tensor("bps", [128, 16], F32))
            bps2 = es0.enter_context(nc.psum_tensor("bps2", [128, 16], F32))
            vps = es0.enter_context(nc.psum_tensor("vps", [128, 512], F32))
            ops_ = es0.enter_context(nc.psum_tensor("ops_", [128, 512], F32))

            sc.op("sync", lambda s: s.dma_start(
                out=b16[:], in_=bq_h.rearrange("(h p) -> h p", p=128)),
                incs=[("ldb", 16)])
            sc.op("sync", lambda s: s.dma_start(
                out=b16b[:], in_=bk_h.rearrange("(h p) -> h p", p=128)),
                incs=[("ldb", 16)])
            t_b16 = sc.tok("ldb")
            sc.op("gpsimd", lambda g: g.dma_start(
                out=brow[:], in_=bv_h.rearrange("(o d) -> o d", o=1)),
                incs=[("lds", 16)])
            sc.op("gpsimd", lambda g: g.dma_start(
                out=brow2[:], in_=bo_h.rearrange("(o d) -> o d", o=1)),
                incs=[("lds", 16)])
            sc.op("gpsimd", lambda g: g.dma_start(
                out=wr_sb[:], in_=wr_h.rearrange("(dc p) e -> p dc e", p=128),
                allow_slow_non_contiguous=True), incs=[("lds", 16)])
            t_brow = sc.tok("lds")

            sc.op("tensor", lambda t: t.transpose(bps[:], b16[:],
                                                  ident_f32[:16, :16]),
                  waits=[t_b16, t_setup_ve])
            sc.op("tensor", lambda t: t.transpose(bps2[:], b16b[:],
                                                  ident_f32[:16, :16]),
                  incs=[(PE, 1)])
            t_btr = sc.tok(PE)
            sc.op("vector", lambda v: v.tensor_copy(bq_sb[:], bps[:]),
                  waits=[t_btr])
            sc.op("vector", lambda v: v.tensor_copy(bk_sb[:], bps2[:]),
                  incs=[(VE, 1)])
            t_bias_ve = sc.tok(VE)
            t_vcp = None
            t_ocp = None
            for j in range(4):
                sc.op("tensor", lambda t, j=j: t.matmul(
                    vps[:], ones_col[:], brow[:, j * 512:(j + 1) * 512],
                    start=True, stop=True),
                    waits=[t_brow, t_vcp] if j == 0 else [t_vcp],
                    incs=[(PE, 1)])
                t_v = sc.tok(PE)
                t_vcp = sc.op("vector", lambda v, j=j: v.tensor_copy(
                    bv_bc[:, j * 512:(j + 1) * 512], vps[:]),
                    waits=[t_v], incs=[(VE, 1)])[0]
                sc.op("tensor", lambda t, j=j: t.matmul(
                    ops_[:], ones_col[:], brow2[:, j * 512:(j + 1) * 512],
                    start=True, stop=True), waits=[t_ocp], incs=[(PE, 1)])
                t_o = sc.tok(PE)
                t_ocp = sc.op("vector", lambda v, j=j: v.tensor_copy(
                    bo_bc[:, j * 512:(j + 1) * 512], ops_[:]),
                    waits=[t_o], incs=[(VE, 1)])[0]
            t_bcast = sc.tok(VE)

        precast_jobs = []

        def precast(dst, src, nelem):
            CH = 128 * 4096
            nch = nelem // CH
            sfl = src.rearrange("a b c -> (a b c)" if len(src.shape) == 3
                                else "a b -> (a b)").rearrange(
                "(n p f) -> n p f", p=128, f=4096)
            dfl = dst.rearrange("a b c -> (a b c)" if len(dst.shape) == 3
                                else "a b -> (a b)").rearrange(
                "(n p f) -> n p f", p=128, f=4096)
            for j in range(nch):
                precast_jobs.append((dfl, sfl, j))

        precast(wgb_hbm, wg_h, E * D * FF)
        precast(wub_hbm, wu_h, E * D * FF)
        precast(wdb_hbm, wd_h, E * FF * D)
        precast(wob_hbm, woT, D * D)
        NCAST = len(precast_jobs)
        t_wcast = ("wcast", 16 * NCAST)

        def emit_precast(n):
            for _ in range(min(n, len(precast_jobs))):
                dfl, sfl, j = precast_jobs.pop(0)
                sc.op("gpsimd", lambda g, j=j, sfl=sfl, dfl=dfl: g.dma_start(
                    out=dfl[j], in_=sfl[j]), incs=[("wcast", 16)])

        esB0 = es.enter_context(contextlib.ExitStack())
        xn_own = esB0.enter_context(nc.sbuf_tensor("xn_own", [128, NDC * TOK],
                                                   BF16))
        t_ln_done_ve = None
        t_ln_done_pe = None
        t_attn = {}

        esB = contextlib.ExitStack()
        with esB:
            xn_part = esB.enter_context(nc.sbuf_tensor("xn_part",
                                                       [128, NDC * TOK], BF16))

            with contextlib.ExitStack() as es1:
                NB1 = 3
                xt = [es1.enter_context(nc.sbuf_tensor(
                    f"xt{i}", [128, D], F32, side="right")) for i in range(NB1)]
                xn_tm = [es1.enter_context(nc.sbuf_tensor(
                    f"xntm{i}", [128, D], BF16, side="right"))
                    for i in range(NB1)]
                bn6 = [es1.enter_context(nc.sbuf_tensor(
                    f"bn6_{i}", [128, 24], F32, side="right"))
                    for i in range(NB1)]
                mv = [es1.enter_context(nc.sbuf_tensor(
                    f"mv{i}", [128, 2], F32, side="right"))
                    for i in range(NB1)]
                lnv = [es1.enter_context(nc.sbuf_tensor(
                    f"lnv{i}", [128, 1], F32, side="right"))
                    for i in range(NB1)]
                rstd = [es1.enter_context(nc.sbuf_tensor(
                    f"rstd{i}", [128, 1], F32, side="right"))
                    for i in range(NB1)]
                tp_ps = [es1.enter_context(
                    nc.psum_tensor(f"tpps{i}", [128, 1024], BF16))
                    for i in range(2)]

                xt_free = [None] * NB1
                xntm_free = [None] * NB1
                xntm_free2 = [None] * NB1
                tpps_free = [t_bcast, t_bcast]
                rstd_tok = [None] * NB1
                ln_tok = [None] * 32
                NT = 32 if p1 else 0

                def stageA(i):
                    src = x_own if i < 16 else x_part
                    row0 = (i % 16) * 128
                    b = i % NB1
                    t_ld = sc.op("sync", lambda s, src=src, row0=row0, b=b:
                                 s.dma_start(out=xt[b][:],
                                             in_=src[row0:row0 + 128, :]),
                                 waits=[xt_free[b]],
                                 incs=[("lx%d" % b, 16)])[0]
                    for g in range(4):
                        sc.op("vector", lambda v, b=b, g=g: v.bn_stats(
                            bn6[b][:, g * 6:(g + 1) * 6],
                            xt[b][:, g * 512:(g + 1) * 512]),
                            waits=[t_ld] if g == 0 else (),
                            incs=[(VE, 1)])
                    t_bn = sc.tok(VE)
                    sc.op("vector", lambda v, b=b: v.bn_aggr(mv[b][:], bn6[b][:]),
                          waits=[t_bn], incs=[(VE, 1)])
                    t_mv = sc.tok(VE)
                    sc.op("scalar", lambda s, b=b: s.activation(
                        lnv[b][:], mv[b][:, 1:2], AF.Ln, bias=eps_sb[:]),
                        waits=[t_mv], incs=[(SE, 1)])
                    t_lnv = sc.tok(SE)
                    sc.op("scalar", lambda s, b=b: s.activation(
                        rstd[b][:], lnv[b][:], AF.Exp, scale=-0.5),
                        waits=[t_lnv], incs=[(SE, 1)])
                    rstd_tok[b] = sc.tok(SE)

                def stageB(i):
                    b = i % NB1
                    dst = xn_own if i < 16 else xn_part
                    tcol = (i % 16) * 128
                    toks = sc.op("vector", lambda v, b=b: v.tensor_scalar(
                        xn_tm[b][:], xt[b][:], mv[b][:, 0:1], rstd[b][:],
                        ALU.subtract, ALU.mult),
                        waits=[rstd_tok[b], xntm_free[b], xntm_free2[b]],
                        incs=[(VE, 1)])
                    xt_free[b] = toks[0]
                    t_xn = toks[0]
                    if i < 16:
                        sc.op("sync", lambda s, b=b, i=i: s.dma_start(
                            out=xn_hbm[i * 128:(i + 1) * 128, :],
                            in_=xn_tm[b][:]),
                            waits=[t_xn], incs=[("xns", 16)])
                    for half in range(2):
                        pb = tp_ps[half]
                        for k in range(8):
                            dc = half * 8 + k
                            sc.op("tensor", lambda t, b=b, dc=dc, pb=pb, k=k:
                                  t.transpose(pb[:, k * 128:(k + 1) * 128],
                                              xn_tm[b][:, dc * 128:(dc + 1) * 128],
                                              ident_bf[:]),
                                  waits=[t_xn, tpps_free[half]] if k == 0 else (),
                                  incs=[(PE, 1)] if k == 7 else ())
                        t_tp = sc.tok(PE)
                        toks = sc.op("vector", lambda v, pb=pb, dst=dst,
                                     half=half, tcol=tcol: v.tensor_copy(
                                         dst[:].rearrange(
                                             "p (dc t) -> p dc t", t=TOK)
                                         [:, half * 8:(half + 1) * 8,
                                          tcol:tcol + 128],
                                         pb[:].rearrange(
                                             "p (k t) -> p k t", t=128)),
                                     waits=[t_tp], incs=[(VE, 1)])
                        tpps_free[half] = toks[0]
                        if half == 1:
                            xntm_free[b] = toks[0]
                            ln_tok[i] = toks[0]
                    if i < 16:
                        xntm_free2[b] = sc.tok("xns")

                for i in range(NT + 2):
                    if i < NT:
                        stageA(i)
                    if i >= 2:
                        stageB(i - 2)
                t_ln_done_ve = sc.tok(VE)
                t_ln_done_pe = sc.tok(PE)

            with contextlib.ExitStack() as es2:
                wq_sb = es2.enter_context(
                    nc.sbuf_tensor("wq_sb", [128, NDC * 128], BF16))
                wk_sb = es2.enter_context(
                    nc.sbuf_tensor("wk_sb", [128, NDC * 128], BF16))
                wv_sb = es2.enter_context(
                    nc.sbuf_tensor("wv_sb", [128, NDC * 128], BF16))
                k_sb = es2.enter_context(nc.sbuf_tensor("k_sb", [128, SEQ], BF16))
                q_sb = es2.enter_context(nc.sbuf_tensor("q_sb", [128, TOK], BF16))
                v_sb = es2.enter_context(nc.sbuf_tensor("v_sb", [128, SEQ], BF16))
                e_sb = [es2.enter_context(
                    nc.sbuf_tensor(f"e_sb{i}", [128, SEQ], BF16))
                    for i in range(2)]
                at_st = [es2.enter_context(
                    nc.sbuf_tensor(f"at_st{i}", [128, TOK], BF16))
                    for i in range(2)]
                rz_sb = es2.enter_context(nc.sbuf_tensor("rz_sb", [128, 128], F32))
                cb_sb = es2.enter_context(
                    nc.sbuf_tensor("cb_sb", [128, NQB * E], F32))
                lg_sb = es2.enter_context(nc.sbuf_tensor("lg_sb", [128, E], F32))
                top_sb = es2.enter_context(nc.sbuf_tensor("top_sb", [128, 8], F32))
                e2_sb = es2.enter_context(nc.sbuf_tensor("e2_sb", [128, 1], F32))
                d21_sb = es2.enter_context(nc.sbuf_tensor("d21_sb", [128, 1], F32))
                w1_sb = es2.enter_context(nc.sbuf_tensor("w1_sb", [128, 1], F32))
                w2_sb = es2.enter_context(nc.sbuf_tensor("w2_sb", [128, 1], F32))
                m1_sb = es2.enter_context(nc.sbuf_tensor("m1_sb", [128, E], F32))
                m2_sb = es2.enter_context(nc.sbuf_tensor("m2_sb", [128, E], F32))
                pos_f = es2.enter_context(nc.sbuf_tensor("pos_f", [128, E], F32))
                msel_bf = es2.enter_context(
                    nc.sbuf_tensor("msel_bf", [128, E], BF16))
                p_ts = es2.enter_context(
                    nc.sbuf_tensor("p_ts", [128, E * 64], BF16))
                s_ps = [es2.enter_context(
                    nc.psum_tensor(f"s_ps{i}", [128, 1024], F32))
                    for i in range(2)]
                z_ps = es2.enter_context(nc.psum_tensor("z_ps", [128, 128], F32))
                a_ps = es2.enter_context(nc.psum_tensor("a_ps", [128, 128], F32))
                kq_ps = [es2.enter_context(
                    nc.psum_tensor(f"kq_ps{i}", [128, 512], F32))
                    for i in range(2)]

                def xn_col(tc):
                    if tc < 16:
                        return xn_own, tc * 128
                    return xn_part, (tc - 16) * 128

                w_free = None
                k_free = None
                q_free = None
                v_free = None
                kq_free = [None, None]
                sps_free = [None, None]
                e_free = [None, None]
                z_free = None
                a_free = None
                rz_free = None
                at_free = [None, None]

                def vch(fn, extra=()):
                    t = sc.op("vector", fn, waits=[sc.tok(VE)] + list(extra),
                              incs=[(VE, 1)])
                    return t[0]

                psp_prev = None
                for h in range(nh if p2 else 0):
                    wl = []
                    for (wsb, whbm, sw) in ((wq_sb, wqT, "swq"),
                                            (wk_sb, wkT, "swk"),
                                            (wv_sb, wvT, "swv")):
                        toks = sc.op("gpsimd", lambda g, wsb=wsb, whbm=whbm, h=h:
                                     g.dma_start(
                                         out=wsb[:],
                                         in_=whbm[:, h * 128:(h + 1) * 128]
                                         .rearrange("(dc p) o -> p dc o", p=128)),
                                     waits=[w_free], incs=[(sw, 16)])
                        wl.append(toks[0])
                    t_wq, t_wk, t_wv = wl
                    emit_precast(10 if h < 15 else 10 ** 6)

                    ev = []
                    for tB in range(8):
                        src = xn_own if tB < 4 else xn_part
                        c0 = (tB % 4) * 512
                        pb = tB % 2
                        lnt = ln_tok[(tB % 4) * 4 + 3 + (16 if tB >= 4 else 0)]
                        for dc in range(NDC):
                            sc.op("tensor", lambda t, src=src, c0=c0, dc=dc, pb=pb:
                                  t.matmul(kq_ps[pb][:],
                                           wk_sb[:, dc * 128:(dc + 1) * 128],
                                           src[:, dc * TOK + c0:dc * TOK + c0 + 512],
                                           start=(dc == 0), stop=(dc == NDC - 1)),
                                  waits=[t_wk, kq_free[pb], lnt]
                                  if dc == 0 else (),
                                  incs=[(PE, 1)] if dc == NDC - 1 else ())
                        t_kmm = sc.tok(PE)
                        toks = sc.op("vector", lambda v, tB=tB, pb=pb, h=h:
                                     v.tensor_scalar(
                                         k_sb[:, tB * 512:(tB + 1) * 512],
                                         kq_ps[pb][:], bk_sb[:, h:h + 1],
                                         None, ALU.add),
                                     waits=[t_kmm, t_bias_ve] +
                                     ([k_free] if tB == 0 else []),
                                     incs=[(VE, 1)])
                        kq_free[pb] = toks[0]
                        ev.append(toks[0])
                    for tB in range(4):
                        c0 = tB * 512
                        pb = tB % 2
                        lnt = ln_tok[tB * 4 + 3]
                        for dc in range(NDC):
                            sc.op("tensor", lambda t, c0=c0, dc=dc, pb=pb:
                                  t.matmul(kq_ps[pb][:],
                                           wq_sb[:, dc * 128:(dc + 1) * 128],
                                           xn_own[:, dc * TOK + c0:
                                                  dc * TOK + c0 + 512],
                                           start=(dc == 0), stop=(dc == NDC - 1)),
                                  waits=([t_wq, kq_free[pb], lnt] if tB == 0
                                         else [kq_free[pb], lnt])
                                  if dc == 0 else (),
                                  incs=[(PE, 1)] if dc == NDC - 1 else ())
                        t_qmm = sc.tok(PE)
                        toks = sc.op("vector", lambda v, tB=tB, pb=pb, h=h:
                                     v.tensor_scalar(
                                         q_sb[:, tB * 512:(tB + 1) * 512],
                                         kq_ps[pb][:], bq_sb[:, h:h + 1],
                                         None, ALU.add),
                                     waits=[t_qmm] + ([q_free] if tB == 0 else []),
                                     incs=[(VE, 1)])
                        kq_free[pb] = toks[0]
                        ev.append(toks[0])
                    for tc in range(NKC):
                        xnt, c0 = xn_col(tc)
                        pb = tc % 2
                        lnt = ln_tok[tc]
                        for dc in range(NDC):
                            sc.op("tensor", lambda t, xnt=xnt, c0=c0, dc=dc, pb=pb:
                                  t.matmul(kq_ps[pb][:, 0:128],
                                           xnt[:, dc * TOK + c0:dc * TOK + c0 + 128],
                                           wv_sb[:, dc * 128:(dc + 1) * 128],
                                           start=(dc == 0), stop=(dc == NDC - 1)),
                                  waits=([t_wv, kq_free[pb], lnt] if tc == 0 else
                                         [kq_free[pb], lnt]) if dc == 0 else (),
                                  incs=[(PE, 1)] if dc == NDC - 1 else ())
                        t_vmm = sc.tok(PE)
                        toks = sc.op("vector", lambda v, tc=tc, pb=pb, h=h:
                                     v.tensor_tensor(
                                         v_sb[:, tc * 128:(tc + 1) * 128],
                                         kq_ps[pb][:, 0:128],
                                         bv_bc[:, h * 128:(h + 1) * 128], ALU.add),
                                     waits=[t_vmm, t_bcast] +
                                     ([v_free] if tc == 0 else []),
                                     incs=[(VE, 1)])
                        kq_free[pb] = toks[0]
                    t_v_ev = sc.tok(VE)
                    t_k_ev = ev[7]
                    t_q_ev = ev[11]

                    if p3:
                        ts = h
                        for dc in range(NDC):
                            sc.op("tensor", lambda t, ts=ts, dc=dc: t.matmul(
                                a_ps[:, 0:8],
                                xn_own[:, dc * TOK + ts * 128:
                                       dc * TOK + ts * 128 + 128],
                                wr_sb[:, dc * E:(dc + 1) * E],
                                start=(dc == 0), stop=(dc == NDC - 1)),
                                waits=[a_free, ln_tok[ts], t_brow]
                                if dc == 0 else (),
                                incs=[(PE, 1)] if dc == NDC - 1 else ())
                        t_r = sc.tok(PE)
                        t_lgcp = vch(lambda v: v.tensor_copy(
                            lg_sb[:], a_ps[:, 0:8]), [t_r])
                        vch(lambda v: v.max(top_sb[:], lg_sb[:]))
                        t_d = vch(lambda v: v.tensor_tensor(
                            d21_sb[:], top_sb[:, 1:2], top_sb[:, 0:1],
                            ALU.subtract))
                        sc.op("scalar", lambda s: s.activation(
                            e2_sb[:], d21_sb[:], AF.Exp),
                            waits=[t_d], incs=[(SE, 1)])
                        t_e2 = sc.tok(SE)
                        vch(lambda v: v.tensor_scalar(
                            w1_sb[:], e2_sb[:], 1.0, None, ALU.add), [t_e2])
                        vch(lambda v: v.reciprocal(w2_sb[:], w1_sb[:]))
                        vch(lambda v: v.tensor_copy(w1_sb[:], w2_sb[:]))
                        vch(lambda v: v.tensor_tensor(
                            w2_sb[:], e2_sb[:], w2_sb[:], ALU.mult))
                        vch(lambda v: v.tensor_scalar(
                            m1_sb[:], lg_sb[:], top_sb[:, 0:1], None,
                            ALU.is_equal))
                        vch(lambda v: v.tensor_scalar(
                            m2_sb[:], lg_sb[:], top_sb[:, 1:2], None,
                            ALU.is_equal))
                        vch(lambda v, ts=ts: v.tensor_scalar(
                            cb_sb[:, ts * E:(ts + 1) * E], m1_sb[:], w1_sb[:],
                            None, ALU.mult))
                        vch(lambda v, ts=ts: v.scalar_tensor_tensor(
                            cb_sb[:, ts * E:(ts + 1) * E], m2_sb[:], w2_sb[:],
                            cb_sb[:, ts * E:(ts + 1) * E], ALU.mult, ALU.add))
                        t_cbf = vch(lambda v, ts=ts: v.tensor_copy(
                            cb_bf[:, ts * E:(ts + 1) * E],
                            cb_sb[:, ts * E:(ts + 1) * E]))
                        vch(lambda v: v.tensor_tensor(
                            m1_sb[:], m1_sb[:], m2_sb[:], ALU.add))
                        vch(lambda v: v.tensor_scalar(
                            m2_sb[:], m1_sb[:], 1.0, None, ALU.min))
                        t_msel = vch(lambda v: v.tensor_copy(
                            msel_bf[:], m2_sb[:]))
                        sc.op("tensor", lambda t: t.matmul(
                            a_ps[:, 8:16], tri_bf[:], msel_bf[:],
                            start=True, stop=True),
                            waits=[t_lgcp, t_msel], incs=[(PE, 1)])
                        t_rank = sc.tok(PE)
                        vch(lambda v: v.tensor_scalar(
                            m1_sb[:], m2_sb[:], -64.0, 64.0, ALU.mult,
                            ALU.add))
                        t_pos = vch(lambda v: v.tensor_tensor(
                            pos_f[:], a_ps[:, 8:16], m1_sb[:], ALU.add),
                            [t_rank])
                        a_free = t_pos
                        for e in range(E):
                            vch(lambda v, e=e: v.tensor_scalar(
                                p_ts[:, e * 64:(e + 1) * 64],
                                iota_col[:, 0:64], pos_f[:, e:e + 1],
                                None, ALU.is_equal),
                                [psp_prev] if e == 0 else [])
                        t_pmask = sc.tok(VE)
                        psp_prev = sc.op("sync", lambda s, ts=ts: s.dma_start(
                            out=p_hbm[ts], in_=p_ts[:]),
                            waits=[t_pmask], incs=[("psp", 16)])[0]


                    pv_done = None
                    for qb in range(NQB):
                        eb = e_sb[qb % 2]
                        exp_toks = []
                        for g in range(4):
                            sp = s_ps[g % 2]
                            for kk in range(8):
                                c = g * 8 + kk
                                sc.op("tensor", lambda t, c=c, kk=kk, sp=sp, qb=qb:
                                      t.matmul(sp[:, kk * 128:(kk + 1) * 128],
                                               k_sb[:, c * 128:(c + 1) * 128],
                                               q_sb[:, qb * 128:(qb + 1) * 128],
                                               start=True, stop=True),
                                      waits=[t_k_ev, t_q_ev, sps_free[g % 2]] +
                                      ([ln_tok[31]] if h == 0 and qb == 0 and
                                       g < 2 else [])
                                      if kk == 0 else (),
                                      incs=[(PE, 1)] if kk == 7 else ())
                            t_smm = sc.tok(PE)
                            toks = sc.op("scalar", lambda s, g=g, sp=sp, eb=eb:
                                         s.activation(
                                             eb[:, g * 1024:(g + 1) * 1024],
                                             sp[:], AF.Exp, scale=SCALE),
                                         waits=[t_smm] +
                                         ([e_free[qb % 2]] if g == 0 else []),
                                         incs=[(SE, 1)])
                            sps_free[g % 2] = toks[0]
                            exp_toks.append(toks[0])
                        for c in range(NKC):
                            wz = []
                            if c % 8 == 0:
                                wz.append(exp_toks[c // 8])
                            if c == 0:
                                wz += [z_free, a_free, t_v_ev]
                            sc.op("tensor", lambda t, c=c, eb=eb: t.matmul(
                                z_ps[:], ones_bf[:],
                                eb[:, c * 128:(c + 1) * 128],
                                start=(c == 0), stop=(c == NKC - 1)),
                                waits=wz,
                                incs=[(PE, 1)] if c == NKC - 1 else ())
                            sc.op("tensor", lambda t, c=c, eb=eb: t.matmul(
                                a_ps[:], v_sb[:, c * 128:(c + 1) * 128],
                                eb[:, c * 128:(c + 1) * 128],
                                start=(c == 0), stop=(c == NKC - 1)),
                                incs=[(PE, 1)] if c == NKC - 1 else ())
                        t_zpv = sc.tok(PE)
                        pv_done = t_zpv
                        toks = sc.op("vector", lambda v: v.reciprocal(
                            rz_sb[:], z_ps[:]),
                            waits=[t_zpv, rz_free], incs=[(VE, 1)])
                        z_free = toks[0]
                        toks = sc.op("vector", lambda v, qb=qb, h=h:
                                     v.tensor_tensor(
                                         at_st[h % 2][:, qb * 128:(qb + 1) * 128],
                                         a_ps[:], rz_sb[:], ALU.mult),
                                     waits=[z_free] +
                                     ([at_free[h % 2]] if qb == 0 else []),
                                     incs=[(VE, 1)])
                        a_free = toks[0]
                        rz_free = toks[0]
                        e_free[qb % 2] = t_zpv
                    t_at_ev = sc.tok(VE)

                    toks = sc.op("sync", lambda s, h=h: s.dma_start(
                        out=attn_hbm[h], in_=at_st[h % 2][:]),
                        waits=[t_at_ev], incs=[("sa%d" % (h % 2), 16)])
                    at_free[h % 2] = toks[0]
                    w_free = t_v_ev
                    w_free = sc.tok(PE) if False else w_free
                    k_free = pv_done
                    q_free = pv_done
                    v_free = pv_done
                t_attn["sa0"] = sc.tok("sa0")
                t_attn["sa1"] = sc.tok("sa1")
                t_attn["pe"] = sc.tok(PE)
                t_attn["se"] = sc.tok(SE)
                t_attn["ve"] = sc.tok(VE)

        esB0.close()
        with contextlib.ExitStack() as es3:
            xng_fm = [es3.enter_context(
                nc.sbuf_tensor(f"xngfm{i}", [128, NDC * CAP], BF16))
                for i in range(2)]
            h_sb = es3.enter_context(
                nc.sbuf_tensor("h_sb", [128, NIC * CAP], BF16))
            wd_sb = es3.enter_context(
                nc.sbuf_tensor("wd_sb", [128, NIC * D], BF16))
            wgu_sb = [[es3.enter_context(
                nc.sbuf_tensor(f"wgu{i}{j}", [128, NDC * 128], BF16))
                for j in range(2)] for i in range(2)]
            y_st = [es3.enter_context(
                nc.sbuf_tensor(f"y_st{i}", [128, D], F32)) for i in range(2)]
            sg_sb = es3.enter_context(nc.sbuf_tensor("sg_sb", [128, 512], BF16))
            xg_tm = [es3.enter_context(
                nc.sbuf_tensor(f"xgtm{i}", [128, D], BF16)) for i in range(2)]
            p_ld = [es3.enter_context(
                nc.sbuf_tensor(f"p_ld{i}", [128, 64], BF16)) for i in range(2)]
            pt_st = es3.enter_context(
                nc.sbuf_tensor("pt_st", [64, 8 * 128], BF16))
            cbg_st = es3.enter_context(
                nc.sbuf_tensor("cbg_st", [64, NQB], F32))
            cbg128 = es3.enter_context(
                nc.sbuf_tensor("cbg128", [128, NCC], F32))
            gu_ps = [es3.enter_context(
                nc.psum_tensor(f"gu_ps{i}", [128, 512], F32)) for i in range(4)]
            y_ps = [es3.enter_context(
                nc.psum_tensor(f"y_ps{i}", [128, 512], F32)) for i in range(2)]
            pt_ps = es3.enter_context(
                nc.psum_tensor("pt_ps", [64, 1024], BF16))
            cbg_ps = es3.enter_context(
                nc.psum_tensor("cbg_ps", [128, 512], F32))

            t_p2 = [t_attn["pe"], t_attn["ve"], t_attn["se"],
                    t_attn["sa0"], t_attn["sa1"], sc.tok("psp")]
            t_xns_all = ("xns", 16 * 16)

            gu_free = [None] * 4
            y_free = [None, None]
            wgu_tok = [[None, None], [None, None]]
            wgu_free = [[None, None], [None, None]]
            wd_tok = None
            wd_free = None
            h_free = None
            sg_free = None
            yst_free = [None, None]
            xg_free = [None, None]
            pl_free = [None, None]
            t_down_last_prev = None
            cbg_free = None
            c128_free = None
            xfm_free = [None, None]

            def load_wgu1(e, ib, i, buf, first):
                wh = wgb_hbm if i == 0 else wub_hbm
                sname = ("g%d" if i == 0 else "u%d") % buf
                toks = sc.op("sync", lambda g, wh=wh, e=e, ib=ib, i=i, buf=buf:
                             g.dma_start(
                                 out=wgu_sb[i][buf][:],
                                 in_=wh[e][:, ib * 128:(ib + 1) * 128]
                                 .rearrange("(dc p) i -> p dc i", p=128)),
                             waits=[wgu_free[i][buf]] +
                             ((t_p2 + [t_wcast]) if first else []),
                             incs=[(sname, 16)])
                return toks[0]

            def load_wd(e, first):
                toks = None
                for ic in range(NIC):
                    toks = sc.op("sync", lambda g, e=e, ic=ic: g.dma_start(
                        out=wd_sb[:, ic * D:(ic + 1) * D],
                        in_=wdb_hbm[e][ic * 128:(ic + 1) * 128, :]),
                        waits=([wd_free] + ((t_p2 + [t_wcast]) if first else []))
                        if ic == 0 else (),
                        incs=[("wdl", 16)])
                return toks[0]

            def prep_load(e, ts, st, first=False):
                tb = ts % 2
                st["t_x"][tb] = sc.op("sync", lambda s, ts=ts, tb=tb:
                                      s.dma_start(
                                          out=xg_tm[tb][:],
                                          in_=xn_hbm[ts * 128:(ts + 1) * 128, :]),
                                      waits=[xg_free[tb]] +
                                      ([t_xns_all] + (t_p2 if first else [])
                                       if ts == 0 else []),
                                      incs=[("xg%d" % tb, 16)])[0]
                st["t_p"][tb] = sc.op("sync", lambda s, e=e, ts=ts, tb=tb:
                                      s.dma_start(
                                          out=p_ld[tb][:],
                                          in_=p_hbm[ts][:, e * 64:(e + 1) * 64]),
                                      waits=[pl_free[tb]],
                                      incs=[("pl%d" % tb, 16)])[0]

            def prep_mm(e, ts, st):
                fmbuf = e % 2
                tb = ts % 2
                gb = 2 * (ts % 2)
                t_x = st["t_x"][tb]
                t_p = st["t_p"][tb]
                for dc in range(NDC):
                    yb = gb + dc // 8
                    sc.op("tensor", lambda t, dc=dc, tb=tb, yb=yb:
                          t.matmul(gu_ps[yb][:, (dc % 8) * 64:
                                             (dc % 8) * 64 + 64],
                                   xg_tm[tb][:, dc * 128:(dc + 1) * 128],
                                   p_ld[tb][:], start=True, stop=True),
                          waits=[t_x, t_p, gu_free[yb]]
                          if dc in (0, 8) else (),
                          incs=[(PE, 1)] if dc == NDC - 1 else ())
                sc.op("tensor", lambda t, ts=ts, tb=tb: t.matmul(
                    cbg_ps[0:64, (ts % 2) * 8:(ts % 2) * 8 + 8], p_ld[tb][:],
                    cb_bf[:, ts * E:(ts + 1) * E],
                    start=True, stop=True),
                    waits=[t_cbf_all, st["cbg_ps_free"][ts % 2]],
                    incs=[(PE, 1)])
                sc.op("tensor", lambda t, ts=ts, tb=tb: t.transpose(
                    pt_ps[:, (ts % 8) * 128:(ts % 8) * 128 + 128],
                    p_ld[tb][:], ident_bf[:]),
                    waits=[st["ptev"]] if ts % 8 == 0 else (),
                    incs=[(PE, 1)])
                t_mm = sc.tok(PE)
                toks = sc.op("vector", lambda v, ts=ts, fmbuf=fmbuf, gb=gb:
                             v.tensor_copy(
                                 xng_fm[fmbuf][:].rearrange(
                                     "p (dc t) -> p dc t", t=CAP)
                                 [:, 0:8, ts * 64:ts * 64 + 64],
                                 gu_ps[gb][:, 0:512].rearrange(
                                     "p (dc t) -> p dc t", t=64)),
                             waits=[t_mm] +
                             ([xfm_free[fmbuf]] if ts == 0 else []),
                             incs=[(VE, 1)])
                xg_free[tb] = toks[0]
                pl_free[tb] = toks[0]
                toks = sc.op("vector", lambda v, ts=ts, fmbuf=fmbuf, gb=gb:
                             v.tensor_copy(
                                 xng_fm[fmbuf][:].rearrange(
                                     "p (dc t) -> p dc t", t=CAP)
                                 [:, 8:16, ts * 64:ts * 64 + 64],
                                 gu_ps[gb + 1][:, 0:512].rearrange(
                                     "p (dc t) -> p dc t", t=64)),
                             incs=[(VE, 1)])
                gu_free[gb + 1] = toks[0]
                toks = sc.op("vector", lambda v, ts=ts, e=e:
                             v.tensor_copy(
                                 cbg_st[:, ts:ts + 1],
                                 cbg_ps[0:64, (ts % 2) * 8 + e:
                                        (ts % 2) * 8 + e + 1]),
                             waits=[cbg_free], incs=[(VE, 1)])
                gu_free[gb] = toks[0]
                st["cbg_ps_free"][ts % 2] = toks[0]
                st["t_cbg"] = toks[0]
                if ts % 8 == 7:
                    toks = sc.op("vector", lambda v: v.tensor_copy(
                        pt_st[:], pt_ps[:]),
                        waits=[st["pt_spill"]], incs=[(VE, 1)])
                    st["ptev"] = toks[0]
                    t_ptev = toks[0]
                    for q in range(8):
                        sc.op("sync", lambda s, e=e, ts=ts, q=q:
                              s.dma_start(
                                  out=pt_hbm[ts - 7 + q]
                                  [:, e * 128:(e + 1) * 128],
                                  in_=pt_st[:, q * 128:(q + 1) * 128]),
                              waits=[t_ptev] if q == 0 else (),
                              incs=[("pts", 16)])
                    st["pt_spill"] = sc.tok("pts")
                st["t_fm"] = sc.tok(VE)
                return st

            pst = {"ptev": None, "pt_spill": None, "t_fm": None,
                   "t_cbg": None, "t_x": [None, None], "t_p": [None, None],
                   "cbg_ps_free": [None, None]}
            if p3:
                wgu_tok[0][0] = load_wgu1(0, 0, 0, 0, True)
                wgu_tok[1][0] = load_wgu1(0, 0, 1, 0, True)
                wd_tok = load_wd(0, True)
                t_cbf_all = t_attn["ve"]
                for ts0 in range(NQB):
                    prep_load(0, ts0, pst, first=(ts0 == 0))
                    prep_mm(0, ts0, pst)

            for e in range(ne if p3 else 0):
                fmbuf = e % 2
                t_fm, t_cbg = pst["t_fm"], pst["t_cbg"]
                t_c1 = sc.op("sync", lambda s: s.dma_start(
                    out=cbg128[0:64, :],
                    in_=cbg_st[:].rearrange("p (j two) -> p j two", two=2)
                    [:, :, 0:1], allow_slow_non_contiguous=True),
                    waits=[t_cbg, c128_free], incs=[("cbs", 16)])[0]
                t_c2 = sc.op("sync", lambda s: s.dma_start(
                    out=cbg128[64:128, :],
                    in_=cbg_st[:].rearrange("p (j two) -> p j two", two=2)
                    [:, :, 1:2], allow_slow_non_contiguous=True),
                    incs=[("cbs", 16)])[0]
                for ib in range(NIC):
                    gib = e * NIC + ib
                    buf = gib % 2
                    nxt = (gib + 1) % 2
                    tg = wgu_tok[0][buf]
                    tu = wgu_tok[1][buf]
                    for tB in range(CAP // 512):
                        pp = (gib * (CAP // 512) + tB) % 2
                        gpsA, gpsB = gu_ps[2 * pp], gu_ps[2 * pp + 1]
                        for dc in range(NDC):
                            sc.op("tensor", lambda t, dc=dc, tB=tB, buf=buf,
                                  gpsA=gpsA, fmbuf=fmbuf: t.matmul(
                                      gpsA[:],
                                      wgu_sb[0][buf][:, dc * 128:(dc + 1) * 128],
                                      xng_fm[fmbuf][:, dc * CAP + tB * 512:
                                                    dc * CAP + tB * 512 + 512],
                                      start=(dc == 0), stop=(dc == NDC - 1)),
                                  waits=[tg, t_fm, gu_free[2 * pp]]
                                  if dc == 0 else (),
                                  incs=[(PE, 1)] if dc == NDC - 1 else ())
                        for dc in range(NDC):
                            sc.op("tensor", lambda t, dc=dc, tB=tB, buf=buf,
                                  gpsB=gpsB, fmbuf=fmbuf: t.matmul(
                                      gpsB[:],
                                      wgu_sb[1][buf][:, dc * 128:(dc + 1) * 128],
                                      xng_fm[fmbuf][:, dc * CAP + tB * 512:
                                                    dc * CAP + tB * 512 + 512],
                                      start=(dc == 0), stop=(dc == NDC - 1)),
                                  waits=[tu, gu_free[2 * pp + 1]]
                                  if dc == 0 else (),
                                  incs=[(PE, 1)] if dc == NDC - 1 else ())
                        t_gu = sc.tok(PE)
                        sc.op("scalar", lambda s, gpsA=gpsA: s.activation(
                            sg_sb[:], gpsA[:], AF.Silu),
                            waits=[t_gu, sg_free], incs=[(SE, 1)])
                        t_sg = sc.tok(SE)
                        toks = sc.op("vector", lambda v, ib=ib, tB=tB, gpsB=gpsB:
                                     v.tensor_tensor(
                                         h_sb[:, ib * CAP + tB * 512:
                                              ib * CAP + tB * 512 + 512],
                                         sg_sb[:], gpsB[:], ALU.mult),
                                     waits=[t_sg, h_free] if tB == 0 and
                                     ib == 0 else [t_sg],
                                     incs=[(VE, 1)])
                        sg_free = toks[0]
                        gu_free[2 * pp] = toks[0]
                        gu_free[2 * pp + 1] = toks[0]
                    if ib + 1 < NIC:
                        wgu_tok[0][nxt] = load_wgu1(e, ib + 1, 0, nxt, False)
                        wgu_tok[1][nxt] = load_wgu1(e, ib + 1, 1, nxt, False)
                    elif e + 1 < ne:
                        wgu_tok[0][nxt] = load_wgu1(e + 1, 0, 0, nxt, False)
                        wgu_tok[1][nxt] = load_wgu1(e + 1, 0, 1, nxt, False)
                    wgu_free[0][buf] = sc.tok(PE)
                    wgu_free[1][buf] = sc.tok(PE)
                t_h_done = sc.tok(VE)
                if debug and e == 0:
                    sc.op("sync", lambda s: s.dma_start(
                        out=dbg_xng[:, :], in_=xng_fm[0][:]),
                        waits=[t_h_done], incs=[("dbg", 16)])
                    sc.op("sync", lambda s: s.dma_start(
                        out=dbg_h2[:, :], in_=h_sb[:]),
                        incs=[("dbg", 16)])

                if e + 1 < ne:
                    wd_free = t_down_last_prev
                    prep_load(e + 1, 0, pst)
                    prep_load(e + 1, 1, pst)
                t_down_last = None
                for tsc in range(NCC):
                    for ob in range(4):
                        pb = (tsc * 4 + ob) % 2
                        for ic in range(NIC):
                            sc.op("tensor", lambda t, tsc=tsc, ob=ob, ic=ic,
                                  pb=pb: t.matmul(
                                      y_ps[pb][:],
                                      h_sb[:, ic * CAP + tsc * 128:
                                           ic * CAP + tsc * 128 + 128],
                                      wd_sb[:, ic * D + ob * 512:
                                            ic * D + ob * 512 + 512],
                                      start=(ic == 0), stop=(ic == NIC - 1)),
                                  waits=[wd_tok, t_h_done, y_free[pb]]
                                  if ic == 0 else (),
                                  incs=[(PE, 1)] if ic == NIC - 1 else ())
                        t_y = sc.tok(PE)
                        toks = sc.op("vector", lambda v, tsc=tsc, ob=ob,
                                     pb=pb: v.tensor_scalar(
                                         y_st[tsc % 2][:, ob * 512:(ob + 1) * 512],
                                         y_ps[pb][:],
                                         cbg128[:, tsc:tsc + 1],
                                         None, ALU.mult),
                                     waits=[t_y, t_c2] +
                                     ([yst_free[tsc % 2]] if ob == 0 else []),
                                     incs=[(VE, 1)])
                        y_free[pb] = toks[0]
                    t_yst = sc.tok(VE)
                    toks = sc.op("gpsimd", lambda g, e=e, tsc=tsc: g.dma_start(
                        out=ycmp_hbm[e][tsc * 128:(tsc + 1) * 128, :],
                        in_=y_st[tsc % 2][:]),
                        waits=[t_yst], incs=[("ysp", 16)])
                    yst_free[tsc % 2] = toks[0]
                    if e + 1 < ne:
                        prep_mm(e + 1, 2 * tsc, pst)
                        prep_mm(e + 1, 2 * tsc + 1, pst)
                        if tsc + 1 < NCC:
                            prep_load(e + 1, 2 * tsc + 2, pst)
                            prep_load(e + 1, 2 * tsc + 3, pst)
                    t_down_last = sc.tok(PE)
                h_free = t_down_last
                c128_free = t_down_last
                cbg_free = t_c2
                xfm_free[fmbuf] = t_down_last
                t_down_last_prev = t_down_last
                if e + 1 < ne:
                    wd_free = t_down_last
                    wd_tok = load_wd(e + 1, False)
            t_ffn = {"pe": sc.tok(PE), "ve": sc.tok(VE), "se": sc.tok(SE)}
            t_ffn["ys"] = [sc.tok("ysp"), sc.tok("pts")]

        with contextlib.ExitStack() as es4:
            atT = es4.enter_context(nc.sbuf_tensor("atT", [128, NH * TOK], BF16))
            wo_sb = es4.enter_context(nc.sbuf_tensor("wo_sb", [128, NDC * D],
                                                     BF16))
            yc_sb = [es4.enter_context(
                nc.sbuf_tensor("yc_sb0", [128, (E // 2) * D], BF16))]
            pt_sb = [es4.enter_context(
                nc.sbuf_tensor(f"pt_sb{i}", [128, (E // 2) * 128], BF16))
                for i in range(2)]
            x_t = es4.enter_context(nc.sbuf_tensor("x_t", [128, D], F32))
            o_sb = es4.enter_context(nc.sbuf_tensor("o_sb", [128, D], F32))
            ya_ps = [es4.enter_context(
                nc.psum_tensor(f"ya_ps{i}", [128, 2048], F32)) for i in range(2)]

            t_p3 = [t_ffn["pe"], t_ffn["ve"], t_ffn["se"]] + t_ffn["ys"]
            t_at = None
            for hh in range(NH if p4 else 0):
                t_at = sc.op("sync", lambda s, hh=hh: s.dma_start(
                    out=atT[:, hh * TOK:(hh + 1) * TOK], in_=attn_hbm[hh]),
                    waits=(t_p3 + [t_attn["sa0"], t_attn["sa1"]])
                    if hh == 0 else (),
                    incs=[("at_", 16)])[0]
            t_wo = None
            for dc in range(NDC if p4 else 0):
                t_wo = sc.op("sync", lambda g, dc=dc: g.dma_start(
                    out=wo_sb[:, dc * D:(dc + 1) * D],
                    in_=wob_hbm[dc * 128:(dc + 1) * 128, :]),
                    waits=t_p3 if dc == 0 else (),
                    incs=[("wo_", 16)])[0]

            yc_free = [None]
            pt_free4 = [None, None]
            xt_free4 = None
            o_free = None
            ya_free = [None, None]
            t_out_sto = None
            for tqc in range(NQB if p4 else 0):
                pset = tqc % 2
                lb = tqc % 2
                yb = 0
                t_pt = None
                for e in range(E):
                    half = (e % 2) * 64
                    pr = e // 2
                    t_pt = sc.op("sync", lambda s, tqc=tqc, lb=lb, e=e,
                                 half=half, pr=pr: s.dma_start(
                                     out=pt_sb[lb][half:half + 64,
                                                   pr * 128:(pr + 1) * 128],
                                     in_=pt_hbm[tqc][:, e * 128:(e + 1) * 128]),
                                 waits=[pt_free4[lb]] if e == 0 else (),
                                 incs=[("ptl", 16)])[0]
                t_yc = None
                for e in range(E):
                    half = (e % 2) * 64
                    pr = e // 2
                    t_yc = sc.op("sync", lambda s, e=e, tqc=tqc, yb=yb,
                                 half=half, pr=pr: s.dma_start(
                                     out=yc_sb[yb][half:half + 64,
                                                   pr * D:(pr + 1) * D],
                                     in_=ycmp_hbm[e][tqc * 64:(tqc + 1) * 64, :]),
                                 waits=[yc_free[yb]] if e == 0 else (),
                                 incs=[("ycl", 16)])[0]
                t_xld = sc.op("sync", lambda s, tqc=tqc: s.dma_start(
                    out=x_t[:], in_=x_own[tqc * 128:(tqc + 1) * 128, :]),
                    waits=[xt_free4], incs=[("xt4", 16)])[0]
                for hh in range(NH):
                    for ob in range(4):
                        sc.op("tensor", lambda t, hh=hh, ob=ob, tqc=tqc,
                              pset=pset: t.matmul(
                                  ya_ps[pset][:, ob * 512:(ob + 1) * 512],
                                  atT[:, hh * TOK + tqc * 128:
                                      hh * TOK + tqc * 128 + 128],
                                  wo_sb[:, hh * D + ob * 512:
                                        hh * D + ob * 512 + 512],
                                  start=(hh == 0), stop=False),
                              waits=[t_at, t_wo, ya_free[pset]]
                              if (hh == 0 and ob == 0) else (),
                              incs=())
                for pr in range(E // 2):
                    for ob in range(4):
                        last = (pr == E // 2 - 1 and ob == 3)
                        sc.op("tensor", lambda t, pr=pr, ob=ob, lb=lb, yb=yb,
                              pset=pset: t.matmul(
                                  ya_ps[pset][:, ob * 512:(ob + 1) * 512],
                                  pt_sb[lb][:, pr * 128:(pr + 1) * 128],
                                  yc_sb[yb][:, pr * D + ob * 512:
                                            pr * D + ob * 512 + 512],
                                  start=False, stop=(pr == E // 2 - 1)),
                              waits=[t_pt, t_yc] if (pr == 0 and ob == 0)
                              else (),
                              incs=[(PE, 1)] if last else ())
                t_ya = sc.tok(PE)
                pt_free4[lb] = t_ya
                yc_free[yb] = t_ya
                sc.op("vector", lambda v: v.tensor_tensor(
                    o_sb[:], x_t[:], bo_bc[:], ALU.add),
                    waits=[t_xld, o_free] + (t_p3 if tqc == 0 else []),
                    incs=[(VE, 1)])
                xt_free4 = sc.tok(VE)
                for ob in range(4):
                    sc.op("vector", lambda v, ob=ob, pset=pset: v.tensor_tensor(
                        o_sb[:, ob * 512:(ob + 1) * 512],
                        o_sb[:, ob * 512:(ob + 1) * 512],
                        ya_ps[pset][:, ob * 512:(ob + 1) * 512], ALU.add),
                        waits=[sc.tok(VE)] + ([t_ya] if ob == 0 else []),
                        incs=[(VE, 1)])
                t_osum = sc.tok(VE)
                ya_free[pset] = t_osum
                t_out_sto = sc.op("sync", lambda s, tqc=tqc: s.dma_start(
                    out=out_h[tqc * 128:(tqc + 1) * 128, :], in_=o_sb[:]),
                    waits=[t_osum], incs=[("so", 16)])[0]
                o_free = t_out_sto

            if sc.counts["so"]:
                sc.op("sync", lambda s: s.nop(), waits=[sc.tok("so")])

        sems = {}
        for name in list(sc.counts):
            sems[name] = es.enter_context(nc.semaphore(name))

        block = es.enter_context(nc.Block())

        def runner(engname):
            def run(eng):
                for engine, fn, waits, incs in sc.ops:
                    if engine != engname:
                        continue
                    for s, v in waits:
                        eng.wait_ge(sems[s], v)
                    inst = fn(eng)
                    for s, n in incs:
                        inst = inst.then_inc(sems[s], n)
            return run

        block.tensor(runner("tensor"))
        block.vector(runner("vector"))
        block.scalar(runner("scalar"))
        block.gpsimd(runner("gpsimd"))
        block.sync(runner("sync"))

    return nc


_CACHE = {}


def _get_nc():
    if "nc" not in _CACHE:
        _CACHE["nc"] = build_nc()
    return _CACHE["nc"]


def kernel(**inputs):
    x = np.asarray(inputs["L0_attn_hidden_states"], dtype=np.float32)
    wq = np.asarray(inputs["L0_attn_wq"], dtype=np.float32)
    wk = np.asarray(inputs["L0_attn_wk"], dtype=np.float32)
    wv = np.asarray(inputs["L0_attn_wv"], dtype=np.float32)
    wo = np.asarray(inputs["L0_attn_wo"], dtype=np.float32)
    shared = {
        "wqT": np.ascontiguousarray(wq.T),
        "wkT": np.ascontiguousarray(wk.T),
        "wvT": np.ascontiguousarray(wv.T),
        "woT": np.ascontiguousarray(wo.T),
        "bq": np.ascontiguousarray(inputs["L0_attn_bq"], dtype=np.float32),
        "bk": np.ascontiguousarray(inputs["L0_attn_bk"], dtype=np.float32),
        "bv": np.ascontiguousarray(inputs["L0_attn_bv"], dtype=np.float32),
        "bo": np.ascontiguousarray(inputs["L0_attn_bo"], dtype=np.float32),
        "wr": np.ascontiguousarray(inputs["L0_ffn_Wr"], dtype=np.float32),
        "wg": np.ascontiguousarray(inputs["L0_ffn_Wg"], dtype=np.float32),
        "wu": np.ascontiguousarray(inputs["L0_ffn_Wu"], dtype=np.float32),
        "wd": np.ascontiguousarray(inputs["L0_ffn_Wd"], dtype=np.float32),
    }
    nc = _get_nc()
    in_maps = []
    for c in range(8):
        b, half = divmod(c, 2)
        m = dict(shared)
        m["x_own"] = np.ascontiguousarray(x[b, half * TOK:(half + 1) * TOK])
        m["x_part"] = np.ascontiguousarray(
            x[b, (1 - half) * TOK:(2 - half) * TOK])
        in_maps.append(m)
    res = run_bass_kernel_spmd(nc, in_maps, core_ids=list(range(8)))
    out = np.empty((4, SEQ, D), np.float32)
    for c in range(8):
        b, half = divmod(c, 2)
        out[b, half * TOK:(half + 1) * TOK] = res.results[c]["out"]
    return out


if __name__ == "__main__":
    nc = build_nc()
    print("BUILD OK, instructions:", nc.n_instructions())


# version 13
# speedup vs baseline: 1.0161x; 1.0161x over previous
import contextlib
import numpy as np

import concourse.bass as bass
import concourse.mybir as mybir
from concourse.bass_utils import run_bass_kernel_spmd
from concourse.bass_core_rust import IndirectOffsetOnAxis

F32 = mybir.dt.float32
BF16 = mybir.dt.bfloat16
AF = mybir.ActivationFunctionType
ALU = mybir.AluOpType
AX = mybir.AxisListType

D = 2048
NDC = 16
NH = 16
TOK = 2048
SEQ = 4096
NKC = 32
NQB = 16
E = 8
FF = 1408
NIC = 11
CAP = 1024
NCC = CAP // 128
SCALE = 1.0 / float(np.sqrt(128.0))
EPS = 1e-5


class Sched:

    def __init__(self):
        self.ops = []
        self.counts = {}

    def sem(self, name):
        self.counts[name] = 0
        return name

    def op(self, engine, fn, waits=(), incs=()):
        ws = []
        for w in waits:
            if w is None:
                continue
            s, v = w
            if v > 0:
                ws.append((s, v))
        self.ops.append((engine, fn, ws, list(incs)))
        out = []
        for s, n in incs:
            self.counts[s] += n
            out.append((s, self.counts[s]))
        return out

    def tok(self, name):
        return (name, self.counts[name])


def build_nc(nh=NH, ne=E, p1=True, p2=True, p3=True, p4=True, debug=False):
    nc = bass.Bass("TRN2", target_bir_lowering=False)
    spill_kind = "ExternalOutput" if debug else "Internal"

    x_own = nc.dram_tensor("x_own", [TOK, D], F32, kind="ExternalInput").ap()
    x_part = nc.dram_tensor("x_part", [TOK, D], F32, kind="ExternalInput").ap()
    wqT = nc.dram_tensor("wqT", [D, D], F32, kind="ExternalInput").ap()
    wkT = nc.dram_tensor("wkT", [D, D], F32, kind="ExternalInput").ap()
    wvT = nc.dram_tensor("wvT", [D, D], F32, kind="ExternalInput").ap()
    woT = nc.dram_tensor("woT", [D, D], F32, kind="ExternalInput").ap()
    bq_h = nc.dram_tensor("bq", [D], F32, kind="ExternalInput").ap()
    bk_h = nc.dram_tensor("bk", [D], F32, kind="ExternalInput").ap()
    bv_h = nc.dram_tensor("bv", [D], F32, kind="ExternalInput").ap()
    bo_h = nc.dram_tensor("bo", [D], F32, kind="ExternalInput").ap()
    wr_h = nc.dram_tensor("wr", [D, E], F32, kind="ExternalInput").ap()
    wg_h = nc.dram_tensor("wg", [E, D, FF], F32, kind="ExternalInput").ap()
    wu_h = nc.dram_tensor("wu", [E, D, FF], F32, kind="ExternalInput").ap()
    wd_h = nc.dram_tensor("wd", [E, FF, D], F32, kind="ExternalInput").ap()
    out_h = nc.dram_tensor("out", [TOK, D], F32, kind="ExternalOutput").ap()

    attn_hbm = nc.dram_tensor("attn_spill", [NH, 128, TOK], BF16,
                              kind=spill_kind).ap()
    xn_hbm = nc.dram_tensor("xn_spill", [TOK, D], BF16, kind=spill_kind).ap()
    wgb_hbm = nc.dram_tensor("wg_bf", [E, D, FF], BF16, kind="Internal").ap()
    wub_hbm = nc.dram_tensor("wu_bf", [E, D, FF], BF16, kind="Internal").ap()
    wdb_hbm = nc.dram_tensor("wd_bf", [E, FF, D], BF16, kind="Internal").ap()
    wob_hbm = nc.dram_tensor("wo_bf", [D, D], BF16, kind="Internal").ap()
    p_hbm = nc.dram_tensor("p_spill", [NQB, 128, E * 64], BF16,
                           kind=spill_kind).ap()
    pt_hbm = nc.dram_tensor("pt_spill", [NQB, 64, E * 128], BF16,
                            kind=spill_kind).ap()
    ycmp_hbm = [nc.dram_tensor("ycmp%d" % e, [CAP, D], BF16,
                               kind=spill_kind).ap() for e in range(E)]
    if debug:
        dbg_xng = nc.dram_tensor("dbg_xng", [128, NDC * CAP], BF16,
                                 kind="ExternalOutput").ap()
        dbg_h2 = nc.dram_tensor("dbg_h2", [128, NIC * CAP], BF16,
                                kind="ExternalOutput").ap()
        dbg_cb = nc.dram_tensor("dbg_cb", [128, NQB * E], F32,
                                kind="ExternalOutput").ap()
        dbg_h = nc.dram_tensor("dbg_h", [128, NIC * TOK], BF16,
                               kind="ExternalOutput").ap()
        dbg_r = nc.dram_tensor("dbg_r", [128, 32], F32,
                               kind="ExternalOutput").ap()
        dbg_lg = nc.dram_tensor("dbg_lg", [128, NQB * E], F32,
                                kind="ExternalOutput").ap()

    sc = Sched()
    for _sname in ["ldb", "lds", "lx0", "lx1", "swq", "swk", "swv",
                    "sa0", "sa1", "g0", "g1", "u0", "u1", "wdl",
                    "wo_", "at_", "yt", "xt4", "so", "dbg",
                    "lx2", "wcast",
                    "xns0", "xns1", "xns2",
                    "psp",
                    "xg0", "xg1",
                    "pl0", "pl1",
                    "pts",
                    "cbs",
                    "ysp0", "ysp1",
                    "ptl",
                    "ycl"]:
        sc.sem(_sname)
    PE = sc.sem("pe")
    VE = sc.sem("ve")
    SE = sc.sem("se")
    GO = sc.sem("go")

    es = contextlib.ExitStack()
    with es:
        ident_bf = es.enter_context(nc.sbuf_tensor("ident_bf", [128, 128], BF16))
        ident_f32 = es.enter_context(nc.sbuf_tensor("ident_f32", [128, 128], F32))
        ones_bf = es.enter_context(nc.sbuf_tensor("ones_bf", [128, 128], BF16))
        ones_col = es.enter_context(nc.sbuf_tensor("ones_col", [1, 128], BF16))
        iota_col = es.enter_context(nc.sbuf_tensor("iota_col", [128, 128], F32))
        iota_row = es.enter_context(nc.sbuf_tensor("iota_row", [128, 1], F32))
        eps_sb = es.enter_context(nc.sbuf_tensor("eps_sb", [128, 1], F32))
        bq_sb = es.enter_context(nc.sbuf_tensor("bq_sb", [128, 16], F32))
        bk_sb = es.enter_context(nc.sbuf_tensor("bk_sb", [128, 16], F32))
        bo_bc = es.enter_context(nc.sbuf_tensor("bo_bc", [128, D], F32))
        bv_bc = es.enter_context(nc.sbuf_tensor("bv_bc", [128, D], BF16))
        wr_sb = es.enter_context(nc.sbuf_tensor("wr_sb", [128, NDC * E], BF16))
        tri_bf = es.enter_context(nc.sbuf_tensor("tri_bf", [128, 128], BF16))
        cb_bf = es.enter_context(nc.sbuf_tensor("cb_bf", [128, NQB * E], BF16))

        sc.op("gpsimd", lambda g: g.iota(iota_col[:], [[1, 128]], base=0,
                                         channel_multiplier=0,
                                         allow_small_or_imprecise_dtypes=True),
              incs=[(GO, 1)])
        sc.op("gpsimd", lambda g: g.iota(iota_row[:], [[0, 1]], base=0,
                                         channel_multiplier=1,
                                         allow_small_or_imprecise_dtypes=True),
              incs=[(GO, 1)])
        t_iota = sc.tok(GO)
        sc.op("vector", lambda v: v.tensor_scalar(
            ident_bf[:], iota_col[:], iota_row[:], None, ALU.is_equal),
            waits=[t_iota], incs=[(VE, 1)])
        sc.op("vector", lambda v: v.tensor_scalar(
            ident_f32[:], iota_col[:], iota_row[:], None, ALU.is_equal),
            incs=[(VE, 1)])
        sc.op("vector", lambda v: v.memset(ones_bf[:], 1.0), incs=[(VE, 1)])
        sc.op("vector", lambda v: v.memset(ones_col[:], 1.0), incs=[(VE, 1)])
        sc.op("vector", lambda v: v.memset(eps_sb[:], EPS), incs=[(VE, 1)])
        sc.op("vector", lambda v: v.tensor_scalar(
            tri_bf[:], iota_col[:], iota_row[:], None, ALU.is_gt),
            incs=[(VE, 1)])
        t_setup_ve = sc.tok(VE)

        with contextlib.ExitStack() as es0:
            b16 = es0.enter_context(nc.sbuf_tensor("b16", [16, 128], F32))
            b16b = es0.enter_context(nc.sbuf_tensor("b16b", [16, 128], F32))
            brow = es0.enter_context(nc.sbuf_tensor("brow", [1, D], BF16))
            brow2 = es0.enter_context(nc.sbuf_tensor("brow2", [1, D], BF16))
            bps = es0.enter_context(nc.psum_tensor("bps", [128, 16], F32))
            bps2 = es0.enter_context(nc.psum_tensor("bps2", [128, 16], F32))
            vps = es0.enter_context(nc.psum_tensor("vps", [128, 512], F32))
            ops_ = es0.enter_context(nc.psum_tensor("ops_", [128, 512], F32))

            sc.op("sync", lambda s: s.dma_start(
                out=b16[:], in_=bq_h.rearrange("(h p) -> h p", p=128)),
                incs=[("ldb", 16)])
            sc.op("sync", lambda s: s.dma_start(
                out=b16b[:], in_=bk_h.rearrange("(h p) -> h p", p=128)),
                incs=[("ldb", 16)])
            t_b16 = sc.tok("ldb")
            sc.op("gpsimd", lambda g: g.dma_start(
                out=brow[:], in_=bv_h.rearrange("(o d) -> o d", o=1)),
                incs=[("lds", 16)])
            sc.op("gpsimd", lambda g: g.dma_start(
                out=brow2[:], in_=bo_h.rearrange("(o d) -> o d", o=1)),
                incs=[("lds", 16)])
            sc.op("gpsimd", lambda g: g.dma_start(
                out=wr_sb[:], in_=wr_h.rearrange("(dc p) e -> p dc e", p=128),
                allow_slow_non_contiguous=True), incs=[("lds", 16)])
            t_brow = sc.tok("lds")

            sc.op("tensor", lambda t: t.transpose(bps[:], b16[:],
                                                  ident_f32[:16, :16]),
                  waits=[t_b16, t_setup_ve])
            sc.op("tensor", lambda t: t.transpose(bps2[:], b16b[:],
                                                  ident_f32[:16, :16]),
                  incs=[(PE, 1)])
            t_btr = sc.tok(PE)
            sc.op("vector", lambda v: v.tensor_copy(bq_sb[:], bps[:]),
                  waits=[t_btr])
            sc.op("vector", lambda v: v.tensor_copy(bk_sb[:], bps2[:]),
                  incs=[(VE, 1)])
            t_bias_ve = sc.tok(VE)
            t_vcp = None
            t_ocp = None
            for j in range(4):
                sc.op("tensor", lambda t, j=j: t.matmul(
                    vps[:], ones_col[:], brow[:, j * 512:(j + 1) * 512],
                    start=True, stop=True),
                    waits=[t_brow, t_vcp] if j == 0 else [t_vcp],
                    incs=[(PE, 1)])
                t_v = sc.tok(PE)
                t_vcp = sc.op("vector", lambda v, j=j: v.tensor_copy(
                    bv_bc[:, j * 512:(j + 1) * 512], vps[:]),
                    waits=[t_v], incs=[(VE, 1)])[0]
                sc.op("tensor", lambda t, j=j: t.matmul(
                    ops_[:], ones_col[:], brow2[:, j * 512:(j + 1) * 512],
                    start=True, stop=True), waits=[t_ocp], incs=[(PE, 1)])
                t_o = sc.tok(PE)
                t_ocp = sc.op("vector", lambda v, j=j: v.tensor_copy(
                    bo_bc[:, j * 512:(j + 1) * 512], ops_[:]),
                    waits=[t_o], incs=[(VE, 1)])[0]
            t_bcast = sc.tok(VE)

        precast_jobs = []

        def precast(dst, src, nelem):
            CH = 128 * 4096
            nch = nelem // CH
            sfl = src.rearrange("a b c -> (a b c)" if len(src.shape) == 3
                                else "a b -> (a b)").rearrange(
                "(n p f) -> n p f", p=128, f=4096)
            dfl = dst.rearrange("a b c -> (a b c)" if len(dst.shape) == 3
                                else "a b -> (a b)").rearrange(
                "(n p f) -> n p f", p=128, f=4096)
            for j in range(nch):
                precast_jobs.append((dfl, sfl, j))

        precast(wgb_hbm, wg_h, E * D * FF)
        precast(wub_hbm, wu_h, E * D * FF)
        precast(wdb_hbm, wd_h, E * FF * D)
        precast(wob_hbm, woT, D * D)
        NCAST = len(precast_jobs)
        t_wcast = ("wcast", 16 * NCAST)

        def emit_precast(n):
            for _ in range(min(n, len(precast_jobs))):
                dfl, sfl, j = precast_jobs.pop(0)
                sc.op("gpsimd", lambda g, j=j, sfl=sfl, dfl=dfl: g.dma_start(
                    out=dfl[j], in_=sfl[j]), incs=[("wcast", 16)])

        esB0 = es.enter_context(contextlib.ExitStack())
        xn_own = esB0.enter_context(nc.sbuf_tensor("xn_own", [128, NDC * TOK],
                                                   BF16))
        t_ln_done_ve = None
        t_ln_done_pe = None
        t_attn = {}

        esB = contextlib.ExitStack()
        with esB:
            xn_part = esB.enter_context(nc.sbuf_tensor("xn_part",
                                                       [128, NDC * TOK], BF16))

            with contextlib.ExitStack() as es1:
                NB1 = 3
                xt = [es1.enter_context(nc.sbuf_tensor(
                    f"xt{i}", [128, D], F32, side="right")) for i in range(NB1)]
                xn_tm = [es1.enter_context(nc.sbuf_tensor(
                    f"xntm{i}", [128, D], BF16, side="right"))
                    for i in range(NB1)]
                bn6 = [es1.enter_context(nc.sbuf_tensor(
                    f"bn6_{i}", [128, 24], F32, side="right"))
                    for i in range(NB1)]
                mv = [es1.enter_context(nc.sbuf_tensor(
                    f"mv{i}", [128, 2], F32, side="right"))
                    for i in range(NB1)]
                lnv = [es1.enter_context(nc.sbuf_tensor(
                    f"lnv{i}", [128, 1], F32, side="right"))
                    for i in range(NB1)]
                rstd = [es1.enter_context(nc.sbuf_tensor(
                    f"rstd{i}", [128, 1], F32, side="right"))
                    for i in range(NB1)]
                tp_ps = [es1.enter_context(
                    nc.psum_tensor(f"tpps{i}", [128, 1024], BF16))
                    for i in range(2)]

                xt_free = [None] * NB1
                xntm_free = [None] * NB1
                xntm_free2 = [None] * NB1
                tpps_free = [t_bcast, t_bcast]
                rstd_tok = [None] * NB1
                ln_tok = [None] * 32
                NT = 32 if p1 else 0

                def stageA(i):
                    src = x_own if i < 16 else x_part
                    row0 = (i % 16) * 128
                    b = i % NB1
                    t_ld = sc.op("sync", lambda s, src=src, row0=row0, b=b:
                                 s.dma_start(out=xt[b][:],
                                             in_=src[row0:row0 + 128, :]),
                                 waits=[xt_free[b]],
                                 incs=[("lx%d" % b, 16)])[0]
                    for g in range(4):
                        sc.op("vector", lambda v, b=b, g=g: v.bn_stats(
                            bn6[b][:, g * 6:(g + 1) * 6],
                            xt[b][:, g * 512:(g + 1) * 512]),
                            waits=[t_ld] if g == 0 else (),
                            incs=[(VE, 1)])
                    t_bn = sc.tok(VE)
                    sc.op("vector", lambda v, b=b: v.bn_aggr(mv[b][:], bn6[b][:]),
                          waits=[t_bn], incs=[(VE, 1)])
                    t_mv = sc.tok(VE)
                    sc.op("scalar", lambda s, b=b: s.activation(
                        lnv[b][:], mv[b][:, 1:2], AF.Ln, bias=eps_sb[:]),
                        waits=[t_mv], incs=[(SE, 1)])
                    t_lnv = sc.tok(SE)
                    sc.op("scalar", lambda s, b=b: s.activation(
                        rstd[b][:], lnv[b][:], AF.Exp, scale=-0.5),
                        waits=[t_lnv], incs=[(SE, 1)])
                    rstd_tok[b] = sc.tok(SE)

                def stageB(i):
                    b = i % NB1
                    dst = xn_own if i < 16 else xn_part
                    tcol = (i % 16) * 128
                    toks = sc.op("vector", lambda v, b=b: v.tensor_scalar(
                        xn_tm[b][:], xt[b][:], mv[b][:, 0:1], rstd[b][:],
                        ALU.subtract, ALU.mult),
                        waits=[rstd_tok[b], xntm_free[b], xntm_free2[b]],
                        incs=[(VE, 1)])
                    xt_free[b] = toks[0]
                    t_xn = toks[0]
                    if i < 16:
                        sc.op("sync", lambda s, b=b, i=i: s.dma_start(
                            out=xn_hbm[i * 128:(i + 1) * 128, :],
                            in_=xn_tm[b][:]),
                            waits=[t_xn], incs=[("xns%d" % b, 16)])
                    for half in range(2):
                        pb = tp_ps[half]
                        for k in range(8):
                            dc = half * 8 + k
                            sc.op("tensor", lambda t, b=b, dc=dc, pb=pb, k=k:
                                  t.transpose(pb[:, k * 128:(k + 1) * 128],
                                              xn_tm[b][:, dc * 128:(dc + 1) * 128],
                                              ident_bf[:]),
                                  waits=[t_xn, tpps_free[half]] if k == 0 else (),
                                  incs=[(PE, 1)] if k == 7 else ())
                        t_tp = sc.tok(PE)
                        toks = sc.op("vector", lambda v, pb=pb, dst=dst,
                                     half=half, tcol=tcol: v.tensor_copy(
                                         dst[:].rearrange(
                                             "p (dc t) -> p dc t", t=TOK)
                                         [:, half * 8:(half + 1) * 8,
                                          tcol:tcol + 128],
                                         pb[:].rearrange(
                                             "p (k t) -> p k t", t=128)),
                                     waits=[t_tp], incs=[(VE, 1)])
                        tpps_free[half] = toks[0]
                        if half == 1:
                            xntm_free[b] = toks[0]
                            ln_tok[i] = toks[0]
                    if i < 16:
                        xntm_free2[b] = sc.tok("xns%d" % b)

                for i in range(NT + 2):
                    if i < NT:
                        stageA(i)
                    if i >= 2:
                        stageB(i - 2)
                t_ln_done_ve = sc.tok(VE)
                t_ln_done_pe = sc.tok(PE)

            with contextlib.ExitStack() as es2:
                wq_sb = es2.enter_context(
                    nc.sbuf_tensor("wq_sb", [128, NDC * 128], BF16))
                wk_sb = es2.enter_context(
                    nc.sbuf_tensor("wk_sb", [128, NDC * 128], BF16))
                wv_sb = es2.enter_context(
                    nc.sbuf_tensor("wv_sb", [128, NDC * 128], BF16))
                k_sb = es2.enter_context(nc.sbuf_tensor("k_sb", [128, SEQ], BF16))
                q_sb = es2.enter_context(nc.sbuf_tensor("q_sb", [128, TOK], BF16))
                v_sb = es2.enter_context(nc.sbuf_tensor("v_sb", [128, SEQ], BF16))
                e_sb = [es2.enter_context(
                    nc.sbuf_tensor(f"e_sb{i}", [128, SEQ], BF16))
                    for i in range(2)]
                at_st = [es2.enter_context(
                    nc.sbuf_tensor(f"at_st{i}", [128, TOK], BF16))
                    for i in range(2)]
                rz_sb = es2.enter_context(nc.sbuf_tensor("rz_sb", [128, 128], F32))
                cb_sb = es2.enter_context(
                    nc.sbuf_tensor("cb_sb", [128, NQB * E], F32))
                lg_sb = es2.enter_context(nc.sbuf_tensor("lg_sb", [128, E], F32))
                top_sb = es2.enter_context(nc.sbuf_tensor("top_sb", [128, 8], F32))
                e2_sb = es2.enter_context(nc.sbuf_tensor("e2_sb", [128, 1], F32))
                d21_sb = es2.enter_context(nc.sbuf_tensor("d21_sb", [128, 1], F32))
                w1_sb = es2.enter_context(nc.sbuf_tensor("w1_sb", [128, 1], F32))
                w2_sb = es2.enter_context(nc.sbuf_tensor("w2_sb", [128, 1], F32))
                m1_sb = es2.enter_context(nc.sbuf_tensor("m1_sb", [128, E], F32))
                m2_sb = es2.enter_context(nc.sbuf_tensor("m2_sb", [128, E], F32))
                pos_f = es2.enter_context(nc.sbuf_tensor("pos_f", [128, E], F32))
                msel_bf = es2.enter_context(
                    nc.sbuf_tensor("msel_bf", [128, E], BF16))
                p_ts = es2.enter_context(
                    nc.sbuf_tensor("p_ts", [128, E * 64], BF16))
                s_ps = [es2.enter_context(
                    nc.psum_tensor(f"s_ps{i}", [128, 1024], F32))
                    for i in range(2)]
                z_ps = es2.enter_context(nc.psum_tensor("z_ps", [128, 128], F32))
                a_ps = es2.enter_context(nc.psum_tensor("a_ps", [128, 128], F32))
                kq_ps = [es2.enter_context(
                    nc.psum_tensor(f"kq_ps{i}", [128, 512], F32))
                    for i in range(2)]

                def xn_col(tc):
                    if tc < 16:
                        return xn_own, tc * 128
                    return xn_part, (tc - 16) * 128

                w_free = None
                k_free = None
                q_free = None
                v_free = None
                kq_free = [None, None]
                sps_free = [None, None]
                e_free = [None, None]
                z_free = None
                a_free = None
                rz_free = None
                at_free = [None, None]

                def vch(fn, extra=()):
                    t = sc.op("vector", fn, waits=[sc.tok(VE)] + list(extra),
                              incs=[(VE, 1)])
                    return t[0]

                psp_prev = None
                for h in range(nh if p2 else 0):
                    wl = []
                    for (wsb, whbm, sw) in ((wq_sb, wqT, "swq"),
                                            (wk_sb, wkT, "swk"),
                                            (wv_sb, wvT, "swv")):
                        toks = sc.op("gpsimd", lambda g, wsb=wsb, whbm=whbm, h=h:
                                     g.dma_start(
                                         out=wsb[:],
                                         in_=whbm[:, h * 128:(h + 1) * 128]
                                         .rearrange("(dc p) o -> p dc o", p=128)),
                                     waits=[w_free], incs=[(sw, 16)])
                        wl.append(toks[0])
                    t_wq, t_wk, t_wv = wl
                    emit_precast(10 if h < 15 else 10 ** 6)

                    ev = []
                    for tB in range(8):
                        src = xn_own if tB < 4 else xn_part
                        c0 = (tB % 4) * 512
                        pb = tB % 2
                        lnt = ln_tok[(tB % 4) * 4 + 3 + (16 if tB >= 4 else 0)]
                        for dc in range(NDC):
                            sc.op("tensor", lambda t, src=src, c0=c0, dc=dc, pb=pb:
                                  t.matmul(kq_ps[pb][:],
                                           wk_sb[:, dc * 128:(dc + 1) * 128],
                                           src[:, dc * TOK + c0:dc * TOK + c0 + 512],
                                           start=(dc == 0), stop=(dc == NDC - 1)),
                                  waits=[t_wk, kq_free[pb], lnt]
                                  if dc == 0 else (),
                                  incs=[(PE, 1)] if dc == NDC - 1 else ())
                        t_kmm = sc.tok(PE)
                        toks = sc.op("vector", lambda v, tB=tB, pb=pb, h=h:
                                     v.tensor_scalar(
                                         k_sb[:, tB * 512:(tB + 1) * 512],
                                         kq_ps[pb][:], bk_sb[:, h:h + 1],
                                         None, ALU.add),
                                     waits=[t_kmm, t_bias_ve] +
                                     ([k_free] if tB == 0 else []),
                                     incs=[(VE, 1)])
                        kq_free[pb] = toks[0]
                        ev.append(toks[0])
                    for tB in range(4):
                        c0 = tB * 512
                        pb = tB % 2
                        lnt = ln_tok[tB * 4 + 3]
                        for dc in range(NDC):
                            sc.op("tensor", lambda t, c0=c0, dc=dc, pb=pb:
                                  t.matmul(kq_ps[pb][:],
                                           wq_sb[:, dc * 128:(dc + 1) * 128],
                                           xn_own[:, dc * TOK + c0:
                                                  dc * TOK + c0 + 512],
                                           start=(dc == 0), stop=(dc == NDC - 1)),
                                  waits=([t_wq, kq_free[pb], lnt] if tB == 0
                                         else [kq_free[pb], lnt])
                                  if dc == 0 else (),
                                  incs=[(PE, 1)] if dc == NDC - 1 else ())
                        t_qmm = sc.tok(PE)
                        toks = sc.op("vector", lambda v, tB=tB, pb=pb, h=h:
                                     v.tensor_scalar(
                                         q_sb[:, tB * 512:(tB + 1) * 512],
                                         kq_ps[pb][:], bq_sb[:, h:h + 1],
                                         None, ALU.add),
                                     waits=[t_qmm] + ([q_free] if tB == 0 else []),
                                     incs=[(VE, 1)])
                        kq_free[pb] = toks[0]
                        ev.append(toks[0])
                    for tc in range(NKC):
                        xnt, c0 = xn_col(tc)
                        pb = tc % 2
                        lnt = ln_tok[tc]
                        for dc in range(NDC):
                            sc.op("tensor", lambda t, xnt=xnt, c0=c0, dc=dc, pb=pb:
                                  t.matmul(kq_ps[pb][:, 0:128],
                                           xnt[:, dc * TOK + c0:dc * TOK + c0 + 128],
                                           wv_sb[:, dc * 128:(dc + 1) * 128],
                                           start=(dc == 0), stop=(dc == NDC - 1)),
                                  waits=([t_wv, kq_free[pb], lnt] if tc == 0 else
                                         [kq_free[pb], lnt]) if dc == 0 else (),
                                  incs=[(PE, 1)] if dc == NDC - 1 else ())
                        t_vmm = sc.tok(PE)
                        toks = sc.op("vector", lambda v, tc=tc, pb=pb, h=h:
                                     v.tensor_tensor(
                                         v_sb[:, tc * 128:(tc + 1) * 128],
                                         kq_ps[pb][:, 0:128],
                                         bv_bc[:, h * 128:(h + 1) * 128], ALU.add),
                                     waits=[t_vmm, t_bcast] +
                                     ([v_free] if tc == 0 else []),
                                     incs=[(VE, 1)])
                        kq_free[pb] = toks[0]
                    t_v_ev = sc.tok(VE)
                    t_k_ev = ev[7]
                    t_q_ev = ev[11]

                    if p3:
                        ts = h
                        for dc in range(NDC):
                            sc.op("tensor", lambda t, ts=ts, dc=dc: t.matmul(
                                a_ps[:, 0:8],
                                xn_own[:, dc * TOK + ts * 128:
                                       dc * TOK + ts * 128 + 128],
                                wr_sb[:, dc * E:(dc + 1) * E],
                                start=(dc == 0), stop=(dc == NDC - 1)),
                                waits=[a_free, ln_tok[ts], t_brow]
                                if dc == 0 else (),
                                incs=[(PE, 1)] if dc == NDC - 1 else ())
                        t_r = sc.tok(PE)
                        t_lgcp = vch(lambda v: v.tensor_copy(
                            lg_sb[:], a_ps[:, 0:8]), [t_r])
                        vch(lambda v: v.max(top_sb[:], lg_sb[:]))
                        t_d = vch(lambda v: v.tensor_tensor(
                            d21_sb[:], top_sb[:, 1:2], top_sb[:, 0:1],
                            ALU.subtract))
                        sc.op("scalar", lambda s: s.activation(
                            e2_sb[:], d21_sb[:], AF.Exp),
                            waits=[t_d], incs=[(SE, 1)])
                        t_e2 = sc.tok(SE)
                        vch(lambda v: v.tensor_scalar(
                            w1_sb[:], e2_sb[:], 1.0, None, ALU.add), [t_e2])
                        vch(lambda v: v.reciprocal(w2_sb[:], w1_sb[:]))
                        vch(lambda v: v.tensor_copy(w1_sb[:], w2_sb[:]))
                        vch(lambda v: v.tensor_tensor(
                            w2_sb[:], e2_sb[:], w2_sb[:], ALU.mult))
                        vch(lambda v: v.tensor_scalar(
                            m1_sb[:], lg_sb[:], top_sb[:, 0:1], None,
                            ALU.is_equal))
                        vch(lambda v: v.tensor_scalar(
                            m2_sb[:], lg_sb[:], top_sb[:, 1:2], None,
                            ALU.is_equal))
                        vch(lambda v, ts=ts: v.tensor_scalar(
                            cb_sb[:, ts * E:(ts + 1) * E], m1_sb[:], w1_sb[:],
                            None, ALU.mult))
                        vch(lambda v, ts=ts: v.scalar_tensor_tensor(
                            cb_sb[:, ts * E:(ts + 1) * E], m2_sb[:], w2_sb[:],
                            cb_sb[:, ts * E:(ts + 1) * E], ALU.mult, ALU.add))
                        t_cbf = vch(lambda v, ts=ts: v.tensor_copy(
                            cb_bf[:, ts * E:(ts + 1) * E],
                            cb_sb[:, ts * E:(ts + 1) * E]))
                        vch(lambda v: v.tensor_tensor(
                            m1_sb[:], m1_sb[:], m2_sb[:], ALU.add))
                        vch(lambda v: v.tensor_scalar(
                            m2_sb[:], m1_sb[:], 1.0, None, ALU.min))
                        t_msel = vch(lambda v: v.tensor_copy(
                            msel_bf[:], m2_sb[:]))
                        sc.op("tensor", lambda t: t.matmul(
                            a_ps[:, 8:16], tri_bf[:], msel_bf[:],
                            start=True, stop=True),
                            waits=[t_lgcp, t_msel], incs=[(PE, 1)])
                        t_rank = sc.tok(PE)
                        vch(lambda v: v.tensor_scalar(
                            m1_sb[:], m2_sb[:], -64.0, 64.0, ALU.mult,
                            ALU.add))
                        t_pos = vch(lambda v: v.tensor_tensor(
                            pos_f[:], a_ps[:, 8:16], m1_sb[:], ALU.add),
                            [t_rank])
                        a_free = t_pos
                        for e in range(E):
                            vch(lambda v, e=e: v.tensor_scalar(
                                p_ts[:, e * 64:(e + 1) * 64],
                                iota_col[:, 0:64], pos_f[:, e:e + 1],
                                None, ALU.is_equal),
                                [psp_prev] if e == 0 else [])
                        t_pmask = sc.tok(VE)
                        psp_prev = sc.op("sync", lambda s, ts=ts: s.dma_start(
                            out=p_hbm[ts], in_=p_ts[:]),
                            waits=[t_pmask], incs=[("psp", 16)])[0]


                    pv_done = None
                    for qb in range(NQB):
                        eb = e_sb[qb % 2]
                        exp_toks = []
                        for g in range(4):
                            sp = s_ps[g % 2]
                            for kk in range(8):
                                c = g * 8 + kk
                                sc.op("tensor", lambda t, c=c, kk=kk, sp=sp, qb=qb:
                                      t.matmul(sp[:, kk * 128:(kk + 1) * 128],
                                               k_sb[:, c * 128:(c + 1) * 128],
                                               q_sb[:, qb * 128:(qb + 1) * 128],
                                               start=True, stop=True),
                                      waits=[t_k_ev, t_q_ev, sps_free[g % 2]] +
                                      ([ln_tok[31]] if h == 0 and qb == 0 and
                                       g < 2 else [])
                                      if kk == 0 else (),
                                      incs=[(PE, 1)] if kk == 7 else ())
                            t_smm = sc.tok(PE)
                            toks = sc.op("scalar", lambda s, g=g, sp=sp, eb=eb:
                                         s.activation(
                                             eb[:, g * 1024:(g + 1) * 1024],
                                             sp[:], AF.Exp, scale=SCALE),
                                         waits=[t_smm] +
                                         ([e_free[qb % 2]] if g == 0 else []),
                                         incs=[(SE, 1)])
                            sps_free[g % 2] = toks[0]
                            exp_toks.append(toks[0])
                        for c in range(NKC):
                            wz = []
                            if c % 8 == 0:
                                wz.append(exp_toks[c // 8])
                            if c == 0:
                                wz += [z_free, a_free, t_v_ev]
                            sc.op("tensor", lambda t, c=c, eb=eb: t.matmul(
                                z_ps[:], ones_bf[:],
                                eb[:, c * 128:(c + 1) * 128],
                                start=(c == 0), stop=(c == NKC - 1)),
                                waits=wz,
                                incs=[(PE, 1)] if c == NKC - 1 else ())
                            sc.op("tensor", lambda t, c=c, eb=eb: t.matmul(
                                a_ps[:], v_sb[:, c * 128:(c + 1) * 128],
                                eb[:, c * 128:(c + 1) * 128],
                                start=(c == 0), stop=(c == NKC - 1)),
                                incs=[(PE, 1)] if c == NKC - 1 else ())
                        t_zpv = sc.tok(PE)
                        pv_done = t_zpv
                        toks = sc.op("vector", lambda v: v.reciprocal(
                            rz_sb[:], z_ps[:]),
                            waits=[t_zpv, rz_free], incs=[(VE, 1)])
                        z_free = toks[0]
                        toks = sc.op("vector", lambda v, qb=qb, h=h:
                                     v.tensor_tensor(
                                         at_st[h % 2][:, qb * 128:(qb + 1) * 128],
                                         a_ps[:], rz_sb[:], ALU.mult),
                                     waits=[z_free] +
                                     ([at_free[h % 2]] if qb == 0 else []),
                                     incs=[(VE, 1)])
                        a_free = toks[0]
                        rz_free = toks[0]
                        e_free[qb % 2] = t_zpv
                    t_at_ev = sc.tok(VE)

                    toks = sc.op("sync", lambda s, h=h: s.dma_start(
                        out=attn_hbm[h], in_=at_st[h % 2][:]),
                        waits=[t_at_ev], incs=[("sa%d" % (h % 2), 16)])
                    at_free[h % 2] = toks[0]
                    w_free = t_v_ev
                    w_free = sc.tok(PE) if False else w_free
                    k_free = pv_done
                    q_free = pv_done
                    v_free = pv_done
                t_attn["sa0"] = sc.tok("sa0")
                t_attn["sa1"] = sc.tok("sa1")
                t_attn["pe"] = sc.tok(PE)
                t_attn["se"] = sc.tok(SE)
                t_attn["ve"] = sc.tok(VE)

        esB0.close()
        with contextlib.ExitStack() as es3:
            xng_fm = [es3.enter_context(
                nc.sbuf_tensor(f"xngfm{i}", [128, NDC * CAP], BF16))
                for i in range(2)]
            h_sb = es3.enter_context(
                nc.sbuf_tensor("h_sb", [128, NIC * CAP], BF16))
            wd_sb = es3.enter_context(
                nc.sbuf_tensor("wd_sb", [128, NIC * D], BF16))
            wgu_sb = [[es3.enter_context(
                nc.sbuf_tensor(f"wgu{i}{j}", [128, NDC * 128], BF16))
                for j in range(2)] for i in range(2)]
            y_st = [es3.enter_context(
                nc.sbuf_tensor(f"y_st{i}", [128, D], F32)) for i in range(2)]
            sg_sb = es3.enter_context(nc.sbuf_tensor("sg_sb", [128, 512], BF16))
            xg_tm = [es3.enter_context(
                nc.sbuf_tensor(f"xgtm{i}", [128, D], BF16)) for i in range(2)]
            p_ld = [es3.enter_context(
                nc.sbuf_tensor(f"p_ld{i}", [128, 64], BF16)) for i in range(2)]
            pt_st = es3.enter_context(
                nc.sbuf_tensor("pt_st", [64, 8 * 128], BF16))
            cbg_st = es3.enter_context(
                nc.sbuf_tensor("cbg_st", [64, NQB], F32))
            cbg128 = es3.enter_context(
                nc.sbuf_tensor("cbg128", [128, NCC], F32))
            gu_ps = [es3.enter_context(
                nc.psum_tensor(f"gu_ps{i}", [128, 512], F32)) for i in range(4)]
            y_ps = [es3.enter_context(
                nc.psum_tensor(f"y_ps{i}", [128, 512], F32)) for i in range(2)]
            pt_ps = es3.enter_context(
                nc.psum_tensor("pt_ps", [64, 1024], BF16))
            cbg_ps = es3.enter_context(
                nc.psum_tensor("cbg_ps", [128, 512], F32))

            t_p2 = [t_attn["pe"], t_attn["ve"], t_attn["se"],
                    t_attn["sa0"], t_attn["sa1"], sc.tok("psp")]
            xns_totals = [sc.tok("xns%d" % b) for b in range(3)]
            t_xns_all = None

            gu_free = [None] * 4
            y_free = [None, None]
            wgu_tok = [[None, None], [None, None]]
            wgu_free = [[None, None], [None, None]]
            wd_tok = None
            wd_free = None
            h_free = None
            sg_free = None
            yst_free = [None, None]
            xg_free = [None, None]
            pl_free = [None, None]
            t_down_last_prev = None
            cbg_free = None
            c128_free = None
            xfm_free = [None, None]

            def load_wgu1(e, ib, i, buf, first):
                wh = wgb_hbm if i == 0 else wub_hbm
                sname = ("g%d" if i == 0 else "u%d") % buf
                toks = sc.op("sync", lambda g, wh=wh, e=e, ib=ib, i=i, buf=buf:
                             g.dma_start(
                                 out=wgu_sb[i][buf][:],
                                 in_=wh[e][:, ib * 128:(ib + 1) * 128]
                                 .rearrange("(dc p) i -> p dc i", p=128)),
                             waits=[wgu_free[i][buf]] +
                             ((t_p2 + [t_wcast]) if first else []),
                             incs=[(sname, 16)])
                return toks[0]

            def load_wd(e, first):
                toks = None
                for ic in range(NIC):
                    toks = sc.op("sync", lambda g, e=e, ic=ic: g.dma_start(
                        out=wd_sb[:, ic * D:(ic + 1) * D],
                        in_=wdb_hbm[e][ic * 128:(ic + 1) * 128, :]),
                        waits=([wd_free] + ((t_p2 + [t_wcast]) if first else []))
                        if ic == 0 else (),
                        incs=[("wdl", 16)])
                return toks[0]

            def prep_load(e, ts, st, first=False):
                tb = ts % 2
                st["t_x"][tb] = sc.op("sync", lambda s, ts=ts, tb=tb:
                                      s.dma_start(
                                          out=xg_tm[tb][:],
                                          in_=xn_hbm[ts * 128:(ts + 1) * 128, :]),
                                      waits=[xg_free[tb]] +
                                      (xns_totals + (t_p2 if first else [])
                                       if ts == 0 else []),
                                      incs=[("xg%d" % tb, 16)])[0]
                st["t_p"][tb] = sc.op("sync", lambda s, e=e, ts=ts, tb=tb:
                                      s.dma_start(
                                          out=p_ld[tb][:],
                                          in_=p_hbm[ts][:, e * 64:(e + 1) * 64]),
                                      waits=[pl_free[tb]],
                                      incs=[("pl%d" % tb, 16)])[0]

            def prep_mm(e, ts, st):
                fmbuf = e % 2
                tb = ts % 2
                gb = 2 * (ts % 2)
                t_x = st["t_x"][tb]
                t_p = st["t_p"][tb]
                for dc in range(NDC):
                    yb = gb + dc // 8
                    sc.op("tensor", lambda t, dc=dc, tb=tb, yb=yb:
                          t.matmul(gu_ps[yb][:, (dc % 8) * 64:
                                             (dc % 8) * 64 + 64],
                                   xg_tm[tb][:, dc * 128:(dc + 1) * 128],
                                   p_ld[tb][:], start=True, stop=True),
                          waits=[t_x, t_p, gu_free[yb]]
                          if dc in (0, 8) else (),
                          incs=[(PE, 1)] if dc == NDC - 1 else ())
                sc.op("tensor", lambda t, ts=ts, tb=tb: t.matmul(
                    cbg_ps[0:64, 0:8], p_ld[tb][:],
                    cb_bf[:, ts * E:(ts + 1) * E],
                    start=True, stop=True),
                    waits=[t_cbf_all, st["cbg_ps_free"][0]],
                    incs=[(PE, 1)])
                sc.op("tensor", lambda t, ts=ts, tb=tb: t.transpose(
                    pt_ps[:, (ts % 8) * 128:(ts % 8) * 128 + 128],
                    p_ld[tb][:], ident_bf[:]),
                    waits=[st["ptev"]] if ts % 8 == 0 else (),
                    incs=[(PE, 1)])
                t_mm = sc.tok(PE)
                toks = sc.op("vector", lambda v, ts=ts, fmbuf=fmbuf, gb=gb:
                             v.tensor_copy(
                                 xng_fm[fmbuf][:].rearrange(
                                     "p (dc t) -> p dc t", t=CAP)
                                 [:, 0:8, ts * 64:ts * 64 + 64],
                                 gu_ps[gb][:, 0:512].rearrange(
                                     "p (dc t) -> p dc t", t=64)),
                             waits=[t_mm] +
                             ([xfm_free[fmbuf]] if ts == 0 else []),
                             incs=[(VE, 1)])
                xg_free[tb] = toks[0]
                pl_free[tb] = toks[0]
                toks = sc.op("vector", lambda v, ts=ts, fmbuf=fmbuf, gb=gb:
                             v.tensor_copy(
                                 xng_fm[fmbuf][:].rearrange(
                                     "p (dc t) -> p dc t", t=CAP)
                                 [:, 8:16, ts * 64:ts * 64 + 64],
                                 gu_ps[gb + 1][:, 0:512].rearrange(
                                     "p (dc t) -> p dc t", t=64)),
                             incs=[(VE, 1)])
                gu_free[gb + 1] = toks[0]
                toks = sc.op("vector", lambda v, ts=ts, e=e:
                             v.tensor_copy(
                                 cbg_st[:, ts:ts + 1],
                                 cbg_ps[0:64, e:e + 1]),
                             waits=[cbg_free], incs=[(VE, 1)])
                gu_free[gb] = toks[0]
                st["cbg_ps_free"][0] = toks[0]
                st["t_cbg"] = toks[0]
                if ts % 8 == 7:
                    toks = sc.op("vector", lambda v: v.tensor_copy(
                        pt_st[:], pt_ps[:]),
                        waits=[st["pt_spill"]], incs=[(VE, 1)])
                    st["ptev"] = toks[0]
                    t_ptev = toks[0]
                    for q in range(8):
                        sc.op("sync", lambda s, e=e, ts=ts, q=q:
                              s.dma_start(
                                  out=pt_hbm[ts - 7 + q]
                                  [:, e * 128:(e + 1) * 128],
                                  in_=pt_st[:, q * 128:(q + 1) * 128]),
                              waits=[t_ptev] if q == 0 else (),
                              incs=[("pts", 16)])
                    st["pt_spill"] = sc.tok("pts")
                st["t_fm"] = sc.tok(VE)
                return st

            pst = {"ptev": None, "pt_spill": None, "t_fm": None,
                   "t_cbg": None, "t_x": [None, None], "t_p": [None, None],
                   "cbg_ps_free": [None, None]}
            if p3:
                wgu_tok[0][0] = load_wgu1(0, 0, 0, 0, True)
                wgu_tok[1][0] = load_wgu1(0, 0, 1, 0, True)
                wd_tok = load_wd(0, True)
                t_cbf_all = t_attn["ve"]
                for ts0 in range(NQB):
                    prep_load(0, ts0, pst, first=(ts0 == 0))
                    prep_mm(0, ts0, pst)

            for e in range(ne if p3 else 0):
                fmbuf = e % 2
                t_fm, t_cbg = pst["t_fm"], pst["t_cbg"]
                t_c1 = sc.op("sync", lambda s: s.dma_start(
                    out=cbg128[0:64, :],
                    in_=cbg_st[:].rearrange("p (j two) -> p j two", two=2)
                    [:, :, 0:1], allow_slow_non_contiguous=True),
                    waits=[t_cbg, c128_free], incs=[("cbs", 16)])[0]
                t_c2 = sc.op("sync", lambda s: s.dma_start(
                    out=cbg128[64:128, :],
                    in_=cbg_st[:].rearrange("p (j two) -> p j two", two=2)
                    [:, :, 1:2], allow_slow_non_contiguous=True),
                    incs=[("cbs", 16)])[0]
                for ib in range(NIC):
                    gib = e * NIC + ib
                    buf = gib % 2
                    nxt = (gib + 1) % 2
                    tg = wgu_tok[0][buf]
                    tu = wgu_tok[1][buf]
                    for tB in range(CAP // 512):
                        pp = (gib * (CAP // 512) + tB) % 2
                        gpsA, gpsB = gu_ps[2 * pp], gu_ps[2 * pp + 1]
                        for dc in range(NDC):
                            sc.op("tensor", lambda t, dc=dc, tB=tB, buf=buf,
                                  gpsA=gpsA, fmbuf=fmbuf: t.matmul(
                                      gpsA[:],
                                      wgu_sb[0][buf][:, dc * 128:(dc + 1) * 128],
                                      xng_fm[fmbuf][:, dc * CAP + tB * 512:
                                                    dc * CAP + tB * 512 + 512],
                                      start=(dc == 0), stop=(dc == NDC - 1)),
                                  waits=[tg, t_fm, gu_free[2 * pp]]
                                  if dc == 0 else (),
                                  incs=[(PE, 1)] if dc == NDC - 1 else ())
                        for dc in range(NDC):
                            sc.op("tensor", lambda t, dc=dc, tB=tB, buf=buf,
                                  gpsB=gpsB, fmbuf=fmbuf: t.matmul(
                                      gpsB[:],
                                      wgu_sb[1][buf][:, dc * 128:(dc + 1) * 128],
                                      xng_fm[fmbuf][:, dc * CAP + tB * 512:
                                                    dc * CAP + tB * 512 + 512],
                                      start=(dc == 0), stop=(dc == NDC - 1)),
                                  waits=[tu, gu_free[2 * pp + 1]]
                                  if dc == 0 else (),
                                  incs=[(PE, 1)] if dc == NDC - 1 else ())
                        t_gu = sc.tok(PE)
                        sc.op("scalar", lambda s, gpsA=gpsA: s.activation(
                            sg_sb[:], gpsA[:], AF.Silu),
                            waits=[t_gu, sg_free], incs=[(SE, 1)])
                        t_sg = sc.tok(SE)
                        toks = sc.op("vector", lambda v, ib=ib, tB=tB, gpsB=gpsB:
                                     v.tensor_tensor(
                                         h_sb[:, ib * CAP + tB * 512:
                                              ib * CAP + tB * 512 + 512],
                                         sg_sb[:], gpsB[:], ALU.mult),
                                     waits=[t_sg, h_free] if tB == 0 and
                                     ib == 0 else [t_sg],
                                     incs=[(VE, 1)])
                        sg_free = toks[0]
                        gu_free[2 * pp] = toks[0]
                        gu_free[2 * pp + 1] = toks[0]
                    if ib + 1 < NIC:
                        wgu_tok[0][nxt] = load_wgu1(e, ib + 1, 0, nxt, False)
                        wgu_tok[1][nxt] = load_wgu1(e, ib + 1, 1, nxt, False)
                    elif e + 1 < ne:
                        wgu_tok[0][nxt] = load_wgu1(e + 1, 0, 0, nxt, False)
                        wgu_tok[1][nxt] = load_wgu1(e + 1, 0, 1, nxt, False)
                    wgu_free[0][buf] = sc.tok(PE)
                    wgu_free[1][buf] = sc.tok(PE)
                t_h_done = sc.tok(VE)
                if debug and e == 0:
                    sc.op("sync", lambda s: s.dma_start(
                        out=dbg_xng[:, :], in_=xng_fm[0][:]),
                        waits=[t_h_done], incs=[("dbg", 16)])
                    sc.op("sync", lambda s: s.dma_start(
                        out=dbg_h2[:, :], in_=h_sb[:]),
                        incs=[("dbg", 16)])

                if e + 1 < ne:
                    wd_free = t_down_last_prev
                    prep_load(e + 1, 0, pst)
                    prep_load(e + 1, 1, pst)
                t_down_last = None
                for tsc in range(NCC):
                    for ob in range(4):
                        pb = (tsc * 4 + ob) % 2
                        for ic in range(NIC):
                            sc.op("tensor", lambda t, tsc=tsc, ob=ob, ic=ic,
                                  pb=pb: t.matmul(
                                      y_ps[pb][:],
                                      h_sb[:, ic * CAP + tsc * 128:
                                           ic * CAP + tsc * 128 + 128],
                                      wd_sb[:, ic * D + ob * 512:
                                            ic * D + ob * 512 + 512],
                                      start=(ic == 0), stop=(ic == NIC - 1)),
                                  waits=[wd_tok, t_h_done, y_free[pb]]
                                  if ic == 0 else (),
                                  incs=[(PE, 1)] if ic == NIC - 1 else ())
                        t_y = sc.tok(PE)
                        toks = sc.op("vector", lambda v, tsc=tsc, ob=ob,
                                     pb=pb: v.tensor_scalar(
                                         y_st[tsc % 2][:, ob * 512:(ob + 1) * 512],
                                         y_ps[pb][:],
                                         cbg128[:, tsc:tsc + 1],
                                         None, ALU.mult),
                                     waits=[t_y, t_c2] +
                                     ([yst_free[tsc % 2]] if ob == 0 else []),
                                     incs=[(VE, 1)])
                        y_free[pb] = toks[0]
                    t_yst = sc.tok(VE)
                    toks = sc.op("gpsimd", lambda g, e=e, tsc=tsc: g.dma_start(
                        out=ycmp_hbm[e][tsc * 128:(tsc + 1) * 128, :],
                        in_=y_st[tsc % 2][:]),
                        waits=[t_yst], incs=[("ysp%d" % (tsc % 2), 16)])
                    yst_free[tsc % 2] = toks[0]
                    if e + 1 < ne:
                        prep_mm(e + 1, 2 * tsc, pst)
                        prep_mm(e + 1, 2 * tsc + 1, pst)
                        if tsc + 1 < NCC:
                            prep_load(e + 1, 2 * tsc + 2, pst)
                            prep_load(e + 1, 2 * tsc + 3, pst)
                    t_down_last = sc.tok(PE)
                h_free = t_down_last
                c128_free = t_down_last
                cbg_free = t_c2
                xfm_free[fmbuf] = t_down_last
                t_down_last_prev = t_down_last
                if e + 1 < ne:
                    wd_free = t_down_last
                    wd_tok = load_wd(e + 1, False)
            t_ffn = {"pe": sc.tok(PE), "ve": sc.tok(VE), "se": sc.tok(SE)}
            t_ffn["ys"] = [sc.tok("ysp0"), sc.tok("ysp1"), sc.tok("pts")]

        with contextlib.ExitStack() as es4:
            atT = es4.enter_context(nc.sbuf_tensor("atT", [128, NH * TOK], BF16))
            wo_sb = es4.enter_context(nc.sbuf_tensor("wo_sb", [128, NDC * D],
                                                     BF16))
            yc_sb = [es4.enter_context(
                nc.sbuf_tensor("yc_sb0", [128, (E // 2) * D], BF16))]
            pt_sb = [es4.enter_context(
                nc.sbuf_tensor(f"pt_sb{i}", [128, (E // 2) * 128], BF16))
                for i in range(2)]
            x_t = es4.enter_context(nc.sbuf_tensor("x_t", [128, D], F32))
            o_sb = es4.enter_context(nc.sbuf_tensor("o_sb", [128, D], F32))
            ya_ps = [es4.enter_context(
                nc.psum_tensor(f"ya_ps{i}", [128, 2048], F32)) for i in range(2)]

            t_p3 = [t_ffn["pe"], t_ffn["ve"], t_ffn["se"]] + t_ffn["ys"]
            t_at = None
            for hh in range(NH if p4 else 0):
                t_at = sc.op("sync", lambda s, hh=hh: s.dma_start(
                    out=atT[:, hh * TOK:(hh + 1) * TOK], in_=attn_hbm[hh]),
                    waits=(t_p3 + [t_attn["sa0"], t_attn["sa1"]])
                    if hh == 0 else (),
                    incs=[("at_", 16)])[0]
            t_wo = None
            for dc in range(NDC if p4 else 0):
                t_wo = sc.op("sync", lambda g, dc=dc: g.dma_start(
                    out=wo_sb[:, dc * D:(dc + 1) * D],
                    in_=wob_hbm[dc * 128:(dc + 1) * 128, :]),
                    waits=t_p3 if dc == 0 else (),
                    incs=[("wo_", 16)])[0]

            yc_free = [None]
            pt_free4 = [None, None]
            xt_free4 = None
            o_free = None
            ya_free = [None, None]
            t_out_sto = None
            for tqc in range(NQB if p4 else 0):
                pset = tqc % 2
                lb = tqc % 2
                yb = 0
                t_pt = None
                for e in range(E):
                    half = (e % 2) * 64
                    pr = e // 2
                    t_pt = sc.op("sync", lambda s, tqc=tqc, lb=lb, e=e,
                                 half=half, pr=pr: s.dma_start(
                                     out=pt_sb[lb][half:half + 64,
                                                   pr * 128:(pr + 1) * 128],
                                     in_=pt_hbm[tqc][:, e * 128:(e + 1) * 128]),
                                 waits=[pt_free4[lb]] if e == 0 else (),
                                 incs=[("ptl", 16)])[0]
                t_yc = None
                for e in range(E):
                    half = (e % 2) * 64
                    pr = e // 2
                    t_yc = sc.op("sync", lambda s, e=e, tqc=tqc, yb=yb,
                                 half=half, pr=pr: s.dma_start(
                                     out=yc_sb[yb][half:half + 64,
                                                   pr * D:(pr + 1) * D],
                                     in_=ycmp_hbm[e][tqc * 64:(tqc + 1) * 64, :]),
                                 waits=[yc_free[yb]] if e == 0 else (),
                                 incs=[("ycl", 16)])[0]
                t_xld = sc.op("sync", lambda s, tqc=tqc: s.dma_start(
                    out=x_t[:], in_=x_own[tqc * 128:(tqc + 1) * 128, :]),
                    waits=[xt_free4], incs=[("xt4", 16)])[0]
                for hh in range(NH):
                    for ob in range(4):
                        sc.op("tensor", lambda t, hh=hh, ob=ob, tqc=tqc,
                              pset=pset: t.matmul(
                                  ya_ps[pset][:, ob * 512:(ob + 1) * 512],
                                  atT[:, hh * TOK + tqc * 128:
                                      hh * TOK + tqc * 128 + 128],
                                  wo_sb[:, hh * D + ob * 512:
                                        hh * D + ob * 512 + 512],
                                  start=(hh == 0), stop=False),
                              waits=[t_at, t_wo, ya_free[pset]]
                              if (hh == 0 and ob == 0) else (),
                              incs=())
                for pr in range(E // 2):
                    for ob in range(4):
                        last = (pr == E // 2 - 1 and ob == 3)
                        sc.op("tensor", lambda t, pr=pr, ob=ob, lb=lb, yb=yb,
                              pset=pset: t.matmul(
                                  ya_ps[pset][:, ob * 512:(ob + 1) * 512],
                                  pt_sb[lb][:, pr * 128:(pr + 1) * 128],
                                  yc_sb[yb][:, pr * D + ob * 512:
                                            pr * D + ob * 512 + 512],
                                  start=False, stop=(pr == E // 2 - 1)),
                              waits=[t_pt, t_yc] if (pr == 0 and ob == 0)
                              else (),
                              incs=[(PE, 1)] if last else ())
                t_ya = sc.tok(PE)
                pt_free4[lb] = t_ya
                yc_free[yb] = t_ya
                sc.op("vector", lambda v: v.tensor_tensor(
                    o_sb[:], x_t[:], bo_bc[:], ALU.add),
                    waits=[t_xld, o_free] + (t_p3 if tqc == 0 else []),
                    incs=[(VE, 1)])
                xt_free4 = sc.tok(VE)
                for ob in range(4):
                    sc.op("vector", lambda v, ob=ob, pset=pset: v.tensor_tensor(
                        o_sb[:, ob * 512:(ob + 1) * 512],
                        o_sb[:, ob * 512:(ob + 1) * 512],
                        ya_ps[pset][:, ob * 512:(ob + 1) * 512], ALU.add),
                        waits=[sc.tok(VE)] + ([t_ya] if ob == 0 else []),
                        incs=[(VE, 1)])
                t_osum = sc.tok(VE)
                ya_free[pset] = t_osum
                t_out_sto = sc.op("sync", lambda s, tqc=tqc: s.dma_start(
                    out=out_h[tqc * 128:(tqc + 1) * 128, :], in_=o_sb[:]),
                    waits=[t_osum], incs=[("so", 16)])[0]
                o_free = t_out_sto

            if sc.counts["so"]:
                sc.op("sync", lambda s: s.nop(), waits=[sc.tok("so")])

        sems = {}
        for name in list(sc.counts):
            sems[name] = es.enter_context(nc.semaphore(name))

        block = es.enter_context(nc.Block())

        def runner(engname):
            def run(eng):
                for engine, fn, waits, incs in sc.ops:
                    if engine != engname:
                        continue
                    for s, v in waits:
                        eng.wait_ge(sems[s], v)
                    inst = fn(eng)
                    for s, n in incs:
                        inst = inst.then_inc(sems[s], n)
            return run

        block.tensor(runner("tensor"))
        block.vector(runner("vector"))
        block.scalar(runner("scalar"))
        block.gpsimd(runner("gpsimd"))
        block.sync(runner("sync"))

    return nc


_CACHE = {}


def _get_nc():
    if "nc" not in _CACHE:
        _CACHE["nc"] = build_nc()
    return _CACHE["nc"]


def kernel(**inputs):
    x = np.asarray(inputs["L0_attn_hidden_states"], dtype=np.float32)
    wq = np.asarray(inputs["L0_attn_wq"], dtype=np.float32)
    wk = np.asarray(inputs["L0_attn_wk"], dtype=np.float32)
    wv = np.asarray(inputs["L0_attn_wv"], dtype=np.float32)
    wo = np.asarray(inputs["L0_attn_wo"], dtype=np.float32)
    shared = {
        "wqT": np.ascontiguousarray(wq.T),
        "wkT": np.ascontiguousarray(wk.T),
        "wvT": np.ascontiguousarray(wv.T),
        "woT": np.ascontiguousarray(wo.T),
        "bq": np.ascontiguousarray(inputs["L0_attn_bq"], dtype=np.float32),
        "bk": np.ascontiguousarray(inputs["L0_attn_bk"], dtype=np.float32),
        "bv": np.ascontiguousarray(inputs["L0_attn_bv"], dtype=np.float32),
        "bo": np.ascontiguousarray(inputs["L0_attn_bo"], dtype=np.float32),
        "wr": np.ascontiguousarray(inputs["L0_ffn_Wr"], dtype=np.float32),
        "wg": np.ascontiguousarray(inputs["L0_ffn_Wg"], dtype=np.float32),
        "wu": np.ascontiguousarray(inputs["L0_ffn_Wu"], dtype=np.float32),
        "wd": np.ascontiguousarray(inputs["L0_ffn_Wd"], dtype=np.float32),
    }
    nc = _get_nc()
    in_maps = []
    for c in range(8):
        b, half = divmod(c, 2)
        m = dict(shared)
        m["x_own"] = np.ascontiguousarray(x[b, half * TOK:(half + 1) * TOK])
        m["x_part"] = np.ascontiguousarray(
            x[b, (1 - half) * TOK:(2 - half) * TOK])
        in_maps.append(m)
    res = run_bass_kernel_spmd(nc, in_maps, core_ids=list(range(8)))
    out = np.empty((4, SEQ, D), np.float32)
    for c in range(8):
        b, half = divmod(c, 2)
        out[b, half * TOK:(half + 1) * TOK] = res.results[c]["out"]
    return out


if __name__ == "__main__":
    nc = build_nc()
    print("BUILD OK, instructions:", nc.n_instructions())


# version 14 (final)
# speedup vs baseline: 1.0188x; 1.0027x over previous
import contextlib
import numpy as np

import concourse.bass as bass
import concourse.mybir as mybir
from concourse.bass_utils import run_bass_kernel_spmd
from concourse.bass_core_rust import IndirectOffsetOnAxis

F32 = mybir.dt.float32
BF16 = mybir.dt.bfloat16
AF = mybir.ActivationFunctionType
ALU = mybir.AluOpType
AX = mybir.AxisListType

D = 2048
NDC = 16
NH = 16
TOK = 2048
SEQ = 4096
NKC = 32
NQB = 16
E = 8
FF = 1408
NIC = 11
CAP = 1024
NCC = CAP // 128
SCALE = 1.0 / float(np.sqrt(128.0))
EPS = 1e-5


class Sched:

    def __init__(self):
        self.ops = []
        self.counts = {}

    def sem(self, name):
        self.counts[name] = 0
        return name

    def op(self, engine, fn, waits=(), incs=()):
        ws = []
        for w in waits:
            if w is None:
                continue
            s, v = w
            if v > 0:
                ws.append((s, v))
        self.ops.append((engine, fn, ws, list(incs)))
        out = []
        for s, n in incs:
            self.counts[s] += n
            out.append((s, self.counts[s]))
        return out

    def tok(self, name):
        return (name, self.counts[name])


def build_nc(nh=NH, ne=E, p1=True, p2=True, p3=True, p4=True, debug=False):
    nc = bass.Bass("TRN2", target_bir_lowering=False)
    spill_kind = "ExternalOutput" if debug else "Internal"

    x_own = nc.dram_tensor("x_own", [TOK, D], F32, kind="ExternalInput").ap()
    x_part = nc.dram_tensor("x_part", [TOK, D], F32, kind="ExternalInput").ap()
    wqT = nc.dram_tensor("wqT", [D, D], F32, kind="ExternalInput").ap()
    wkT = nc.dram_tensor("wkT", [D, D], F32, kind="ExternalInput").ap()
    wvT = nc.dram_tensor("wvT", [D, D], F32, kind="ExternalInput").ap()
    woT = nc.dram_tensor("woT", [D, D], F32, kind="ExternalInput").ap()
    bq_h = nc.dram_tensor("bq", [D], F32, kind="ExternalInput").ap()
    bk_h = nc.dram_tensor("bk", [D], F32, kind="ExternalInput").ap()
    bv_h = nc.dram_tensor("bv", [D], F32, kind="ExternalInput").ap()
    bo_h = nc.dram_tensor("bo", [D], F32, kind="ExternalInput").ap()
    wr_h = nc.dram_tensor("wr", [D, E], F32, kind="ExternalInput").ap()
    wg_h = nc.dram_tensor("wg", [E, D, FF], F32, kind="ExternalInput").ap()
    wu_h = nc.dram_tensor("wu", [E, D, FF], F32, kind="ExternalInput").ap()
    wd_h = nc.dram_tensor("wd", [E, FF, D], F32, kind="ExternalInput").ap()
    out_h = nc.dram_tensor("out", [TOK, D], F32, kind="ExternalOutput").ap()

    attn_hbm = nc.dram_tensor("attn_spill", [NH, 128, TOK], BF16,
                              kind=spill_kind).ap()
    xn_hbm = nc.dram_tensor("xn_spill", [TOK, D], BF16, kind=spill_kind).ap()
    wgb_hbm = nc.dram_tensor("wg_bf", [E, D, FF], BF16, kind="Internal").ap()
    wub_hbm = nc.dram_tensor("wu_bf", [E, D, FF], BF16, kind="Internal").ap()
    wdb_hbm = nc.dram_tensor("wd_bf", [E, FF, D], BF16, kind="Internal").ap()
    wob_hbm = nc.dram_tensor("wo_bf", [D, D], BF16, kind="Internal").ap()
    p_hbm = nc.dram_tensor("p_spill", [NQB, 128, E * 64], BF16,
                           kind=spill_kind).ap()
    pt_hbm = nc.dram_tensor("pt_spill", [NQB, 64, E * 128], BF16,
                            kind=spill_kind).ap()
    ycmp_hbm = [nc.dram_tensor("ycmp%d" % e, [CAP, D], BF16,
                               kind=spill_kind).ap() for e in range(E)]
    if debug:
        dbg_xng = nc.dram_tensor("dbg_xng", [128, NDC * CAP], BF16,
                                 kind="ExternalOutput").ap()
        dbg_h2 = nc.dram_tensor("dbg_h2", [128, NIC * CAP], BF16,
                                kind="ExternalOutput").ap()
        dbg_cb = nc.dram_tensor("dbg_cb", [128, NQB * E], F32,
                                kind="ExternalOutput").ap()
        dbg_h = nc.dram_tensor("dbg_h", [128, NIC * TOK], BF16,
                               kind="ExternalOutput").ap()
        dbg_r = nc.dram_tensor("dbg_r", [128, 32], F32,
                               kind="ExternalOutput").ap()
        dbg_lg = nc.dram_tensor("dbg_lg", [128, NQB * E], F32,
                                kind="ExternalOutput").ap()

    sc = Sched()
    for _sname in ["ldb", "lds", "lx0", "lx1", "swq", "swk", "swv",
                    "sa0", "sa1", "g0", "g1", "u0", "u1", "wdl",
                    "wo_", "at_", "yt", "xt4", "so", "dbg",
                    "lx2", "wcast",
                    "xns0", "xns1", "xns2",
                    "psp",
                    "xg0", "xg1",
                    "pl0", "pl1",
                    "pts",
                    "cbs",
                    "ysp0", "ysp1",
                    "ptl",
                    "ycl"]:
        sc.sem(_sname)
    PE = sc.sem("pe")
    VE = sc.sem("ve")
    SE = sc.sem("se")
    GO = sc.sem("go")

    es = contextlib.ExitStack()
    with es:
        ident_bf = es.enter_context(nc.sbuf_tensor("ident_bf", [128, 128], BF16))
        ident_f32 = es.enter_context(nc.sbuf_tensor("ident_f32", [128, 128], F32))
        ones_bf = es.enter_context(nc.sbuf_tensor("ones_bf", [128, 128], BF16))
        ones_col = es.enter_context(nc.sbuf_tensor("ones_col", [1, 128], BF16))
        iota_col = es.enter_context(nc.sbuf_tensor("iota_col", [128, 128], F32))
        iota_row = es.enter_context(nc.sbuf_tensor("iota_row", [128, 1], F32))
        eps_sb = es.enter_context(nc.sbuf_tensor("eps_sb", [128, 1], F32))
        bq_sb = es.enter_context(nc.sbuf_tensor("bq_sb", [128, 16], F32))
        bk_sb = es.enter_context(nc.sbuf_tensor("bk_sb", [128, 16], F32))
        bo_bc = es.enter_context(nc.sbuf_tensor("bo_bc", [128, D], F32))
        bv_bc = es.enter_context(nc.sbuf_tensor("bv_bc", [128, D], BF16))
        wr_sb = es.enter_context(nc.sbuf_tensor("wr_sb", [128, NDC * E], BF16))
        tri_bf = es.enter_context(nc.sbuf_tensor("tri_bf", [128, 128], BF16))
        cb_bf = es.enter_context(nc.sbuf_tensor("cb_bf", [128, NQB * E], BF16))

        sc.op("gpsimd", lambda g: g.iota(iota_col[:], [[1, 128]], base=0,
                                         channel_multiplier=0,
                                         allow_small_or_imprecise_dtypes=True),
              incs=[(GO, 1)])
        sc.op("gpsimd", lambda g: g.iota(iota_row[:], [[0, 1]], base=0,
                                         channel_multiplier=1,
                                         allow_small_or_imprecise_dtypes=True),
              incs=[(GO, 1)])
        t_iota = sc.tok(GO)
        sc.op("vector", lambda v: v.tensor_scalar(
            ident_bf[:], iota_col[:], iota_row[:], None, ALU.is_equal),
            waits=[t_iota], incs=[(VE, 1)])
        sc.op("vector", lambda v: v.tensor_scalar(
            ident_f32[:], iota_col[:], iota_row[:], None, ALU.is_equal),
            incs=[(VE, 1)])
        sc.op("vector", lambda v: v.memset(ones_bf[:], 1.0), incs=[(VE, 1)])
        sc.op("vector", lambda v: v.memset(ones_col[:], 1.0), incs=[(VE, 1)])
        sc.op("vector", lambda v: v.memset(eps_sb[:], EPS), incs=[(VE, 1)])
        sc.op("vector", lambda v: v.tensor_scalar(
            tri_bf[:], iota_col[:], iota_row[:], None, ALU.is_gt),
            incs=[(VE, 1)])
        t_setup_ve = sc.tok(VE)

        with contextlib.ExitStack() as es0:
            b16 = es0.enter_context(nc.sbuf_tensor("b16", [16, 128], F32))
            b16b = es0.enter_context(nc.sbuf_tensor("b16b", [16, 128], F32))
            brow = es0.enter_context(nc.sbuf_tensor("brow", [1, D], BF16))
            brow2 = es0.enter_context(nc.sbuf_tensor("brow2", [1, D], BF16))
            bps = es0.enter_context(nc.psum_tensor("bps", [128, 16], F32))
            bps2 = es0.enter_context(nc.psum_tensor("bps2", [128, 16], F32))
            vps = es0.enter_context(nc.psum_tensor("vps", [128, 512], F32))
            ops_ = es0.enter_context(nc.psum_tensor("ops_", [128, 512], F32))

            sc.op("sync", lambda s: s.dma_start(
                out=b16[:], in_=bq_h.rearrange("(h p) -> h p", p=128)),
                incs=[("ldb", 16)])
            sc.op("sync", lambda s: s.dma_start(
                out=b16b[:], in_=bk_h.rearrange("(h p) -> h p", p=128)),
                incs=[("ldb", 16)])
            t_b16 = sc.tok("ldb")
            sc.op("gpsimd", lambda g: g.dma_start(
                out=brow[:], in_=bv_h.rearrange("(o d) -> o d", o=1)),
                incs=[("lds", 16)])
            sc.op("gpsimd", lambda g: g.dma_start(
                out=brow2[:], in_=bo_h.rearrange("(o d) -> o d", o=1)),
                incs=[("lds", 16)])
            sc.op("gpsimd", lambda g: g.dma_start(
                out=wr_sb[:], in_=wr_h.rearrange("(dc p) e -> p dc e", p=128),
                allow_slow_non_contiguous=True), incs=[("lds", 16)])
            t_brow = sc.tok("lds")

            sc.op("tensor", lambda t: t.transpose(bps[:], b16[:],
                                                  ident_f32[:16, :16]),
                  waits=[t_b16, t_setup_ve])
            sc.op("tensor", lambda t: t.transpose(bps2[:], b16b[:],
                                                  ident_f32[:16, :16]),
                  incs=[(PE, 1)])
            t_btr = sc.tok(PE)
            sc.op("vector", lambda v: v.tensor_copy(bq_sb[:], bps[:]),
                  waits=[t_btr])
            sc.op("vector", lambda v: v.tensor_copy(bk_sb[:], bps2[:]),
                  incs=[(VE, 1)])
            t_bias_ve = sc.tok(VE)
            t_vcp = None
            t_ocp = None
            for j in range(4):
                sc.op("tensor", lambda t, j=j: t.matmul(
                    vps[:], ones_col[:], brow[:, j * 512:(j + 1) * 512],
                    start=True, stop=True),
                    waits=[t_brow, t_vcp] if j == 0 else [t_vcp],
                    incs=[(PE, 1)])
                t_v = sc.tok(PE)
                t_vcp = sc.op("vector", lambda v, j=j: v.tensor_copy(
                    bv_bc[:, j * 512:(j + 1) * 512], vps[:]),
                    waits=[t_v], incs=[(VE, 1)])[0]
                sc.op("tensor", lambda t, j=j: t.matmul(
                    ops_[:], ones_col[:], brow2[:, j * 512:(j + 1) * 512],
                    start=True, stop=True), waits=[t_ocp], incs=[(PE, 1)])
                t_o = sc.tok(PE)
                t_ocp = sc.op("vector", lambda v, j=j: v.tensor_copy(
                    bo_bc[:, j * 512:(j + 1) * 512], ops_[:]),
                    waits=[t_o], incs=[(VE, 1)])[0]
            t_bcast = sc.tok(VE)

        precast_jobs = []

        def precast(dst, src, nelem):
            CH = 128 * 4096
            nch = nelem // CH
            sfl = src.rearrange("a b c -> (a b c)" if len(src.shape) == 3
                                else "a b -> (a b)").rearrange(
                "(n p f) -> n p f", p=128, f=4096)
            dfl = dst.rearrange("a b c -> (a b c)" if len(dst.shape) == 3
                                else "a b -> (a b)").rearrange(
                "(n p f) -> n p f", p=128, f=4096)
            for j in range(nch):
                precast_jobs.append((dfl, sfl, j))

        precast(wgb_hbm, wg_h, E * D * FF)
        precast(wub_hbm, wu_h, E * D * FF)
        precast(wdb_hbm, wd_h, E * FF * D)
        precast(wob_hbm, woT, D * D)
        NCAST = len(precast_jobs)
        t_wcast = ("wcast", 16 * NCAST)

        def emit_precast(n):
            for _ in range(min(n, len(precast_jobs))):
                dfl, sfl, j = precast_jobs.pop(0)
                sc.op("gpsimd", lambda g, j=j, sfl=sfl, dfl=dfl: g.dma_start(
                    out=dfl[j], in_=sfl[j]), incs=[("wcast", 16)])

        esB0 = es.enter_context(contextlib.ExitStack())
        xn_own = esB0.enter_context(nc.sbuf_tensor("xn_own", [128, NDC * TOK],
                                                   BF16))
        t_ln_done_ve = None
        t_ln_done_pe = None
        t_attn = {}

        esB = contextlib.ExitStack()
        with esB:
            xn_part = esB.enter_context(nc.sbuf_tensor("xn_part",
                                                       [128, NDC * TOK], BF16))

            with contextlib.ExitStack() as es1:
                NB1 = 3
                xt = [es1.enter_context(nc.sbuf_tensor(
                    f"xt{i}", [128, D], F32, side="right")) for i in range(NB1)]
                xn_tm = [es1.enter_context(nc.sbuf_tensor(
                    f"xntm{i}", [128, D], BF16, side="right"))
                    for i in range(NB1)]
                bn6 = [es1.enter_context(nc.sbuf_tensor(
                    f"bn6_{i}", [128, 24], F32, side="right"))
                    for i in range(NB1)]
                mv = [es1.enter_context(nc.sbuf_tensor(
                    f"mv{i}", [128, 2], F32, side="right"))
                    for i in range(NB1)]
                lnv = [es1.enter_context(nc.sbuf_tensor(
                    f"lnv{i}", [128, 1], F32, side="right"))
                    for i in range(NB1)]
                rstd = [es1.enter_context(nc.sbuf_tensor(
                    f"rstd{i}", [128, 1], F32, side="right"))
                    for i in range(NB1)]
                tp_ps = [es1.enter_context(
                    nc.psum_tensor(f"tpps{i}", [128, 1024], BF16))
                    for i in range(2)]

                xt_free = [None] * NB1
                xntm_free = [None] * NB1
                xntm_free2 = [None] * NB1
                tpps_free = [t_bcast, t_bcast]
                rstd_tok = [None] * NB1
                ln_tok = [None] * 32
                NT = 32 if p1 else 0

                def stageA(i):
                    src = x_own if i < 16 else x_part
                    row0 = (i % 16) * 128
                    b = i % NB1
                    t_ld = sc.op("sync", lambda s, src=src, row0=row0, b=b:
                                 s.dma_start(out=xt[b][:],
                                             in_=src[row0:row0 + 128, :]),
                                 waits=[xt_free[b]],
                                 incs=[("lx%d" % b, 16)])[0]
                    for g in range(4):
                        sc.op("vector", lambda v, b=b, g=g: v.bn_stats(
                            bn6[b][:, g * 6:(g + 1) * 6],
                            xt[b][:, g * 512:(g + 1) * 512]),
                            waits=[t_ld] if g == 0 else (),
                            incs=[(VE, 1)])
                    t_bn = sc.tok(VE)
                    sc.op("vector", lambda v, b=b: v.bn_aggr(mv[b][:], bn6[b][:]),
                          waits=[t_bn], incs=[(VE, 1)])
                    t_mv = sc.tok(VE)
                    sc.op("scalar", lambda s, b=b: s.activation(
                        lnv[b][:], mv[b][:, 1:2], AF.Ln, bias=eps_sb[:]),
                        waits=[t_mv], incs=[(SE, 1)])
                    t_lnv = sc.tok(SE)
                    sc.op("scalar", lambda s, b=b: s.activation(
                        rstd[b][:], lnv[b][:], AF.Exp, scale=-0.5),
                        waits=[t_lnv], incs=[(SE, 1)])
                    rstd_tok[b] = sc.tok(SE)

                def stageB(i):
                    b = i % NB1
                    dst = xn_own if i < 16 else xn_part
                    tcol = (i % 16) * 128
                    toks = sc.op("vector", lambda v, b=b: v.tensor_scalar(
                        xn_tm[b][:], xt[b][:], mv[b][:, 0:1], rstd[b][:],
                        ALU.subtract, ALU.mult),
                        waits=[rstd_tok[b], xntm_free[b], xntm_free2[b]],
                        incs=[(VE, 1)])
                    xt_free[b] = toks[0]
                    t_xn = toks[0]
                    if i < 16:
                        sc.op("sync", lambda s, b=b, i=i: s.dma_start(
                            out=xn_hbm[i * 128:(i + 1) * 128, :],
                            in_=xn_tm[b][:]),
                            waits=[t_xn], incs=[("xns%d" % b, 16)])
                    for half in range(2):
                        pb = tp_ps[half]
                        for k in range(8):
                            dc = half * 8 + k
                            sc.op("tensor", lambda t, b=b, dc=dc, pb=pb, k=k:
                                  t.transpose(pb[:, k * 128:(k + 1) * 128],
                                              xn_tm[b][:, dc * 128:(dc + 1) * 128],
                                              ident_bf[:]),
                                  waits=[t_xn, tpps_free[half]] if k == 0 else (),
                                  incs=[(PE, 1)] if k == 7 else ())
                        t_tp = sc.tok(PE)
                        toks = sc.op("vector", lambda v, pb=pb, dst=dst,
                                     half=half, tcol=tcol: v.tensor_copy(
                                         dst[:].rearrange(
                                             "p (dc t) -> p dc t", t=TOK)
                                         [:, half * 8:(half + 1) * 8,
                                          tcol:tcol + 128],
                                         pb[:].rearrange(
                                             "p (k t) -> p k t", t=128)),
                                     waits=[t_tp], incs=[(VE, 1)])
                        tpps_free[half] = toks[0]
                        if half == 1:
                            xntm_free[b] = toks[0]
                            ln_tok[i] = toks[0]
                    if i < 16:
                        xntm_free2[b] = sc.tok("xns%d" % b)

                for i in range(NT + 2):
                    if i < NT:
                        stageA(i)
                    if i >= 2:
                        stageB(i - 2)
                t_ln_done_ve = sc.tok(VE)
                t_ln_done_pe = sc.tok(PE)

            with contextlib.ExitStack() as es2:
                wq_sb = es2.enter_context(
                    nc.sbuf_tensor("wq_sb", [128, NDC * 128], BF16))
                wk_sb = es2.enter_context(
                    nc.sbuf_tensor("wk_sb", [128, NDC * 128], BF16))
                wv_sb = es2.enter_context(
                    nc.sbuf_tensor("wv_sb", [128, NDC * 128], BF16))
                k_sb = es2.enter_context(nc.sbuf_tensor("k_sb", [128, SEQ], BF16))
                q_sb = es2.enter_context(nc.sbuf_tensor("q_sb", [128, TOK], BF16))
                v_sb = es2.enter_context(nc.sbuf_tensor("v_sb", [128, SEQ], BF16))
                e_sb = [es2.enter_context(
                    nc.sbuf_tensor(f"e_sb{i}", [128, SEQ], BF16))
                    for i in range(2)]
                at_st = [es2.enter_context(
                    nc.sbuf_tensor(f"at_st{i}", [128, TOK], BF16))
                    for i in range(2)]
                rz_sb = es2.enter_context(nc.sbuf_tensor("rz_sb", [128, 128], F32))
                cb_sb = es2.enter_context(
                    nc.sbuf_tensor("cb_sb", [128, NQB * E], F32))
                lg_sb = es2.enter_context(nc.sbuf_tensor("lg_sb", [128, E], F32))
                top_sb = es2.enter_context(nc.sbuf_tensor("top_sb", [128, 8], F32))
                e2_sb = es2.enter_context(nc.sbuf_tensor("e2_sb", [128, 1], F32))
                d21_sb = es2.enter_context(nc.sbuf_tensor("d21_sb", [128, 1], F32))
                w1_sb = es2.enter_context(nc.sbuf_tensor("w1_sb", [128, 1], F32))
                w2_sb = es2.enter_context(nc.sbuf_tensor("w2_sb", [128, 1], F32))
                m1_sb = es2.enter_context(nc.sbuf_tensor("m1_sb", [128, E], F32))
                m2_sb = es2.enter_context(nc.sbuf_tensor("m2_sb", [128, E], F32))
                pos_f = es2.enter_context(nc.sbuf_tensor("pos_f", [128, E], F32))
                msel_bf = es2.enter_context(
                    nc.sbuf_tensor("msel_bf", [128, E], BF16))
                p_ts = es2.enter_context(
                    nc.sbuf_tensor("p_ts", [128, E * 64], BF16))
                s_ps = [es2.enter_context(
                    nc.psum_tensor(f"s_ps{i}", [128, 1024], F32))
                    for i in range(2)]
                z_ps = es2.enter_context(nc.psum_tensor("z_ps", [128, 128], F32))
                a_ps = es2.enter_context(nc.psum_tensor("a_ps", [128, 128], F32))
                kq_ps = [es2.enter_context(
                    nc.psum_tensor(f"kq_ps{i}", [128, 512], F32))
                    for i in range(2)]

                def xn_col(tc):
                    if tc < 16:
                        return xn_own, tc * 128
                    return xn_part, (tc - 16) * 128

                w_free = None
                k_free = None
                q_free = None
                v_free = None
                kq_free = [None, None]
                sps_free = [None, None]
                e_free = [None, None]
                z_free = None
                a_free = None
                rz_free = None
                at_free = [None, None]

                def vch(fn, extra=()):
                    t = sc.op("vector", fn, waits=[sc.tok(VE)] + list(extra),
                              incs=[(VE, 1)])
                    return t[0]

                psp_prev = None
                for h in range(nh if p2 else 0):
                    wl = []
                    for (wsb, whbm, sw) in ((wq_sb, wqT, "swq"),
                                            (wk_sb, wkT, "swk"),
                                            (wv_sb, wvT, "swv")):
                        toks = sc.op("gpsimd", lambda g, wsb=wsb, whbm=whbm, h=h:
                                     g.dma_start(
                                         out=wsb[:],
                                         in_=whbm[:, h * 128:(h + 1) * 128]
                                         .rearrange("(dc p) o -> p dc o", p=128)),
                                     waits=[w_free], incs=[(sw, 16)])
                        wl.append(toks[0])
                    t_wq, t_wk, t_wv = wl
                    emit_precast(10 if h < 15 else 10 ** 6)

                    ev = []
                    for tB in range(8):
                        src = xn_own if tB < 4 else xn_part
                        c0 = (tB % 4) * 512
                        pb = tB % 2
                        lnt = ln_tok[(tB % 4) * 4 + 3 + (16 if tB >= 4 else 0)]
                        for dc in range(NDC):
                            sc.op("tensor", lambda t, src=src, c0=c0, dc=dc, pb=pb:
                                  t.matmul(kq_ps[pb][:],
                                           wk_sb[:, dc * 128:(dc + 1) * 128],
                                           src[:, dc * TOK + c0:dc * TOK + c0 + 512],
                                           start=(dc == 0), stop=(dc == NDC - 1)),
                                  waits=[t_wk, kq_free[pb], lnt]
                                  if dc == 0 else (),
                                  incs=[(PE, 1)] if dc == NDC - 1 else ())
                        t_kmm = sc.tok(PE)
                        toks = sc.op("scalar", lambda s, tB=tB, pb=pb, h=h:
                                     s.activation(k_sb[:, tB * 512:(tB + 1) * 512],
                                                  kq_ps[pb][:], AF.Identity,
                                                  bias=bk_sb[:, h:h + 1]),
                                     waits=[t_kmm, t_bias_ve] +
                                     ([k_free] if tB == 0 else []),
                                     incs=[(SE, 1)])
                        kq_free[pb] = toks[0]
                        ev.append(toks[0])
                    for tB in range(4):
                        c0 = tB * 512
                        pb = tB % 2
                        lnt = ln_tok[tB * 4 + 3]
                        for dc in range(NDC):
                            sc.op("tensor", lambda t, c0=c0, dc=dc, pb=pb:
                                  t.matmul(kq_ps[pb][:],
                                           wq_sb[:, dc * 128:(dc + 1) * 128],
                                           xn_own[:, dc * TOK + c0:
                                                  dc * TOK + c0 + 512],
                                           start=(dc == 0), stop=(dc == NDC - 1)),
                                  waits=([t_wq, kq_free[pb], lnt] if tB == 0
                                         else [kq_free[pb], lnt])
                                  if dc == 0 else (),
                                  incs=[(PE, 1)] if dc == NDC - 1 else ())
                        t_qmm = sc.tok(PE)
                        toks = sc.op("scalar", lambda s, tB=tB, pb=pb, h=h:
                                     s.activation(q_sb[:, tB * 512:(tB + 1) * 512],
                                                  kq_ps[pb][:], AF.Identity,
                                                  bias=bq_sb[:, h:h + 1]),
                                     waits=[t_qmm] + ([q_free] if tB == 0 else []),
                                     incs=[(SE, 1)])
                        kq_free[pb] = toks[0]
                        ev.append(toks[0])
                    for tc in range(NKC):
                        xnt, c0 = xn_col(tc)
                        pb = tc % 2
                        lnt = ln_tok[tc]
                        for dc in range(NDC):
                            sc.op("tensor", lambda t, xnt=xnt, c0=c0, dc=dc, pb=pb:
                                  t.matmul(kq_ps[pb][:, 0:128],
                                           xnt[:, dc * TOK + c0:dc * TOK + c0 + 128],
                                           wv_sb[:, dc * 128:(dc + 1) * 128],
                                           start=(dc == 0), stop=(dc == NDC - 1)),
                                  waits=([t_wv, kq_free[pb], lnt] if tc == 0 else
                                         [kq_free[pb], lnt]) if dc == 0 else (),
                                  incs=[(PE, 1)] if dc == NDC - 1 else ())
                        t_vmm = sc.tok(PE)
                        toks = sc.op("vector", lambda v, tc=tc, pb=pb, h=h:
                                     v.tensor_tensor(
                                         v_sb[:, tc * 128:(tc + 1) * 128],
                                         kq_ps[pb][:, 0:128],
                                         bv_bc[:, h * 128:(h + 1) * 128], ALU.add),
                                     waits=[t_vmm, t_bcast] +
                                     ([v_free] if tc == 0 else []),
                                     incs=[(VE, 1)])
                        kq_free[pb] = toks[0]
                    t_v_ev = sc.tok(VE)
                    t_k_ev = ev[7]
                    t_q_ev = ev[11]

                    pv_done = None
                    for qb in range(NQB):
                        eb = e_sb[qb % 2]
                        exp_toks = []
                        for g in range(4):
                            sp = s_ps[g % 2]
                            for kk in range(8):
                                c = g * 8 + kk
                                sc.op("tensor", lambda t, c=c, kk=kk, sp=sp, qb=qb:
                                      t.matmul(sp[:, kk * 128:(kk + 1) * 128],
                                               k_sb[:, c * 128:(c + 1) * 128],
                                               q_sb[:, qb * 128:(qb + 1) * 128],
                                               start=True, stop=True),
                                      waits=[t_k_ev, t_q_ev, sps_free[g % 2]] +
                                      ([ln_tok[31]] if h == 0 and qb == 0 and
                                       g < 2 else [])
                                      if kk == 0 else (),
                                      incs=[(PE, 1)] if kk == 7 else ())
                            t_smm = sc.tok(PE)
                            toks = sc.op("scalar", lambda s, g=g, sp=sp, eb=eb:
                                         s.activation(
                                             eb[:, g * 1024:(g + 1) * 1024],
                                             sp[:], AF.Exp, scale=SCALE),
                                         waits=[t_smm] +
                                         ([e_free[qb % 2]] if g == 0 else []),
                                         incs=[(SE, 1)])
                            sps_free[g % 2] = toks[0]
                            exp_toks.append(toks[0])
                        for c in range(NKC):
                            wz = []
                            if c % 8 == 0:
                                wz.append(exp_toks[c // 8])
                            if c == 0:
                                wz += [z_free, a_free, t_v_ev]
                            sc.op("tensor", lambda t, c=c, eb=eb: t.matmul(
                                z_ps[:], ones_bf[:],
                                eb[:, c * 128:(c + 1) * 128],
                                start=(c == 0), stop=(c == NKC - 1)),
                                waits=wz,
                                incs=[(PE, 1)] if c == NKC - 1 else ())
                            sc.op("tensor", lambda t, c=c, eb=eb: t.matmul(
                                a_ps[:], v_sb[:, c * 128:(c + 1) * 128],
                                eb[:, c * 128:(c + 1) * 128],
                                start=(c == 0), stop=(c == NKC - 1)),
                                incs=[(PE, 1)] if c == NKC - 1 else ())
                        t_zpv = sc.tok(PE)
                        pv_done = t_zpv
                        toks = sc.op("vector", lambda v: v.reciprocal(
                            rz_sb[:], z_ps[:]),
                            waits=[t_zpv, rz_free], incs=[(VE, 1)])
                        z_free = toks[0]
                        toks = sc.op("vector", lambda v, qb=qb, h=h:
                                     v.tensor_tensor(
                                         at_st[h % 2][:, qb * 128:(qb + 1) * 128],
                                         a_ps[:], rz_sb[:], ALU.mult),
                                     waits=[z_free] +
                                     ([at_free[h % 2]] if qb == 0 else []),
                                     incs=[(VE, 1)])
                        a_free = toks[0]
                        rz_free = toks[0]
                        e_free[qb % 2] = t_zpv
                        if qb == 2:
                            if p3:
                                ts = h
                                for dc in range(NDC):
                                    sc.op("tensor", lambda t, ts=ts, dc=dc: t.matmul(
                                        kq_ps[0][:, 0:8],
                                        xn_own[:, dc * TOK + ts * 128:
                                               dc * TOK + ts * 128 + 128],
                                        wr_sb[:, dc * E:(dc + 1) * E],
                                        start=(dc == 0), stop=(dc == NDC - 1)),
                                        waits=[kq_free[0], ln_tok[ts], t_brow]
                                        if dc == 0 else (),
                                        incs=[(PE, 1)] if dc == NDC - 1 else ())
                                t_r = sc.tok(PE)
                                t_lgcp = vch(lambda v: v.tensor_copy(
                                    lg_sb[:], kq_ps[0][:, 0:8]), [t_r])
                                vch(lambda v: v.max(top_sb[:], lg_sb[:]))
                                t_d = vch(lambda v: v.tensor_tensor(
                                    d21_sb[:], top_sb[:, 1:2], top_sb[:, 0:1],
                                    ALU.subtract))
                                sc.op("scalar", lambda s: s.activation(
                                    e2_sb[:], d21_sb[:], AF.Exp),
                                    waits=[t_d], incs=[(SE, 1)])
                                t_e2 = sc.tok(SE)
                                vch(lambda v: v.tensor_scalar(
                                    w1_sb[:], e2_sb[:], 1.0, None, ALU.add), [t_e2])
                                vch(lambda v: v.reciprocal(w2_sb[:], w1_sb[:]))
                                vch(lambda v: v.tensor_copy(w1_sb[:], w2_sb[:]))
                                vch(lambda v: v.tensor_tensor(
                                    w2_sb[:], e2_sb[:], w2_sb[:], ALU.mult))
                                vch(lambda v: v.tensor_scalar(
                                    m1_sb[:], lg_sb[:], top_sb[:, 0:1], None,
                                    ALU.is_equal))
                                vch(lambda v: v.tensor_scalar(
                                    m2_sb[:], lg_sb[:], top_sb[:, 1:2], None,
                                    ALU.is_equal))
                                vch(lambda v, ts=ts: v.tensor_scalar(
                                    cb_sb[:, ts * E:(ts + 1) * E], m1_sb[:], w1_sb[:],
                                    None, ALU.mult))
                                vch(lambda v, ts=ts: v.scalar_tensor_tensor(
                                    cb_sb[:, ts * E:(ts + 1) * E], m2_sb[:], w2_sb[:],
                                    cb_sb[:, ts * E:(ts + 1) * E], ALU.mult, ALU.add))
                                t_cbf = vch(lambda v, ts=ts: v.tensor_copy(
                                    cb_bf[:, ts * E:(ts + 1) * E],
                                    cb_sb[:, ts * E:(ts + 1) * E]))
                                vch(lambda v: v.tensor_tensor(
                                    m1_sb[:], m1_sb[:], m2_sb[:], ALU.add))
                                vch(lambda v: v.tensor_scalar(
                                    m2_sb[:], m1_sb[:], 1.0, None, ALU.min))
                                t_msel = vch(lambda v: v.tensor_copy(
                                    msel_bf[:], m2_sb[:]))
                                sc.op("tensor", lambda t: t.matmul(
                                    kq_ps[0][:, 8:16], tri_bf[:], msel_bf[:],
                                    start=True, stop=True),
                                    waits=[t_lgcp, t_msel], incs=[(PE, 1)])
                                t_rank = sc.tok(PE)
                                vch(lambda v: v.tensor_scalar(
                                    m1_sb[:], m2_sb[:], -64.0, 64.0, ALU.mult,
                                    ALU.add))
                                t_pos = vch(lambda v: v.tensor_tensor(
                                    pos_f[:], kq_ps[0][:, 8:16], m1_sb[:], ALU.add),
                                    [t_rank])
                                kq_free[0] = t_pos
                                for e in range(E):
                                    vch(lambda v, e=e: v.tensor_scalar(
                                        p_ts[:, e * 64:(e + 1) * 64],
                                        iota_col[:, 0:64], pos_f[:, e:e + 1],
                                        None, ALU.is_equal),
                                        [psp_prev] if e == 0 else [])
                                t_pmask = sc.tok(VE)
                                psp_prev = sc.op("sync", lambda s, ts=ts: s.dma_start(
                                    out=p_hbm[ts], in_=p_ts[:]),
                                    waits=[t_pmask], incs=[("psp", 16)])[0]


                    t_at_ev = sc.tok(VE)

                    toks = sc.op("sync", lambda s, h=h: s.dma_start(
                        out=attn_hbm[h], in_=at_st[h % 2][:]),
                        waits=[t_at_ev], incs=[("sa%d" % (h % 2), 16)])
                    at_free[h % 2] = toks[0]
                    w_free = t_v_ev
                    w_free = sc.tok(PE) if False else w_free
                    k_free = pv_done
                    q_free = pv_done
                    v_free = pv_done
                t_attn["sa0"] = sc.tok("sa0")
                t_attn["sa1"] = sc.tok("sa1")
                t_attn["pe"] = sc.tok(PE)
                t_attn["se"] = sc.tok(SE)
                t_attn["ve"] = sc.tok(VE)

        esB0.close()
        with contextlib.ExitStack() as es3:
            xng_fm = [es3.enter_context(
                nc.sbuf_tensor(f"xngfm{i}", [128, NDC * CAP], BF16))
                for i in range(2)]
            h_sb = es3.enter_context(
                nc.sbuf_tensor("h_sb", [128, NIC * CAP], BF16))
            wd_sb = es3.enter_context(
                nc.sbuf_tensor("wd_sb", [128, NIC * D], BF16))
            wgu_sb = [[es3.enter_context(
                nc.sbuf_tensor(f"wgu{i}{j}", [128, NDC * 128], BF16))
                for j in range(2)] for i in range(2)]
            y_st = [es3.enter_context(
                nc.sbuf_tensor(f"y_st{i}", [128, D], F32)) for i in range(2)]
            sg_sb = es3.enter_context(nc.sbuf_tensor("sg_sb", [128, 512], BF16))
            xg_tm = [es3.enter_context(
                nc.sbuf_tensor(f"xgtm{i}", [128, D], BF16)) for i in range(2)]
            p_ld = [es3.enter_context(
                nc.sbuf_tensor(f"p_ld{i}", [128, 64], BF16)) for i in range(2)]
            pt_st = es3.enter_context(
                nc.sbuf_tensor("pt_st", [64, 8 * 128], BF16))
            cbg_st = es3.enter_context(
                nc.sbuf_tensor("cbg_st", [64, NQB], F32))
            cbg128 = es3.enter_context(
                nc.sbuf_tensor("cbg128", [128, NCC], F32))
            gu_ps = [es3.enter_context(
                nc.psum_tensor(f"gu_ps{i}", [128, 512], F32)) for i in range(4)]
            y_ps = [es3.enter_context(
                nc.psum_tensor(f"y_ps{i}", [128, 512], F32)) for i in range(2)]
            pt_ps = es3.enter_context(
                nc.psum_tensor("pt_ps", [64, 1024], BF16))
            cbg_ps = es3.enter_context(
                nc.psum_tensor("cbg_ps", [128, 512], F32))

            t_p2 = [t_attn["pe"], t_attn["ve"], t_attn["se"],
                    t_attn["sa0"], t_attn["sa1"], sc.tok("psp")]
            xns_totals = [sc.tok("xns%d" % b) for b in range(3)]
            t_xns_all = None

            gu_free = [None] * 4
            y_free = [None, None]
            wgu_tok = [[None, None], [None, None]]
            wgu_free = [[None, None], [None, None]]
            wd_tok = None
            wd_free = None
            h_free = None
            sg_free = None
            yst_free = [None, None]
            xg_free = [None, None]
            pl_free = [None, None]
            t_down_last_prev = None
            cbg_free = None
            c128_free = None
            xfm_free = [None, None]

            def load_wgu1(e, ib, i, buf, first):
                wh = wgb_hbm if i == 0 else wub_hbm
                sname = ("g%d" if i == 0 else "u%d") % buf
                toks = sc.op("sync", lambda g, wh=wh, e=e, ib=ib, i=i, buf=buf:
                             g.dma_start(
                                 out=wgu_sb[i][buf][:],
                                 in_=wh[e][:, ib * 128:(ib + 1) * 128]
                                 .rearrange("(dc p) i -> p dc i", p=128)),
                             waits=[wgu_free[i][buf]] +
                             ((t_p2 + [t_wcast]) if first else []),
                             incs=[(sname, 16)])
                return toks[0]

            def load_wd(e, first):
                toks = None
                for ic in range(NIC):
                    toks = sc.op("sync", lambda g, e=e, ic=ic: g.dma_start(
                        out=wd_sb[:, ic * D:(ic + 1) * D],
                        in_=wdb_hbm[e][ic * 128:(ic + 1) * 128, :]),
                        waits=([wd_free] + ((t_p2 + [t_wcast]) if first else []))
                        if ic == 0 else (),
                        incs=[("wdl", 16)])
                return toks[0]

            def prep_load(e, ts, st, first=False):
                tb = ts % 2
                st["t_x"][tb] = sc.op("sync", lambda s, ts=ts, tb=tb:
                                      s.dma_start(
                                          out=xg_tm[tb][:],
                                          in_=xn_hbm[ts * 128:(ts + 1) * 128, :]),
                                      waits=[xg_free[tb]] +
                                      (xns_totals + (t_p2 if first else [])
                                       if ts == 0 else []),
                                      incs=[("xg%d" % tb, 16)])[0]
                st["t_p"][tb] = sc.op("sync", lambda s, e=e, ts=ts, tb=tb:
                                      s.dma_start(
                                          out=p_ld[tb][:],
                                          in_=p_hbm[ts][:, e * 64:(e + 1) * 64]),
                                      waits=[pl_free[tb]],
                                      incs=[("pl%d" % tb, 16)])[0]

            def prep_mm(e, ts, st):
                fmbuf = e % 2
                tb = ts % 2
                gb = 2 * (ts % 2)
                t_x = st["t_x"][tb]
                t_p = st["t_p"][tb]
                for dc in range(NDC):
                    yb = gb + dc // 8
                    sc.op("tensor", lambda t, dc=dc, tb=tb, yb=yb:
                          t.matmul(gu_ps[yb][:, (dc % 8) * 64:
                                             (dc % 8) * 64 + 64],
                                   xg_tm[tb][:, dc * 128:(dc + 1) * 128],
                                   p_ld[tb][:], start=True, stop=True),
                          waits=[t_x, t_p, gu_free[yb]]
                          if dc in (0, 8) else (),
                          incs=[(PE, 1)] if dc == NDC - 1 else ())
                sc.op("tensor", lambda t, ts=ts, tb=tb: t.matmul(
                    cbg_ps[0:64, 0:8], p_ld[tb][:],
                    cb_bf[:, ts * E:(ts + 1) * E],
                    start=True, stop=True),
                    waits=[t_cbf_all, st["cbg_ps_free"][0]],
                    incs=[(PE, 1)])
                sc.op("tensor", lambda t, ts=ts, tb=tb: t.transpose(
                    pt_ps[:, (ts % 8) * 128:(ts % 8) * 128 + 128],
                    p_ld[tb][:], ident_bf[:]),
                    waits=[st["ptev"]] if ts % 8 == 0 else (),
                    incs=[(PE, 1)])
                t_mm = sc.tok(PE)
                toks = sc.op("vector", lambda v, ts=ts, fmbuf=fmbuf, gb=gb:
                             v.tensor_copy(
                                 xng_fm[fmbuf][:].rearrange(
                                     "p (dc t) -> p dc t", t=CAP)
                                 [:, 0:8, ts * 64:ts * 64 + 64],
                                 gu_ps[gb][:, 0:512].rearrange(
                                     "p (dc t) -> p dc t", t=64)),
                             waits=[t_mm] +
                             ([xfm_free[fmbuf]] if ts == 0 else []),
                             incs=[(VE, 1)])
                xg_free[tb] = toks[0]
                pl_free[tb] = toks[0]
                toks = sc.op("vector", lambda v, ts=ts, fmbuf=fmbuf, gb=gb:
                             v.tensor_copy(
                                 xng_fm[fmbuf][:].rearrange(
                                     "p (dc t) -> p dc t", t=CAP)
                                 [:, 8:16, ts * 64:ts * 64 + 64],
                                 gu_ps[gb + 1][:, 0:512].rearrange(
                                     "p (dc t) -> p dc t", t=64)),
                             incs=[(VE, 1)])
                gu_free[gb + 1] = toks[0]
                toks = sc.op("vector", lambda v, ts=ts, e=e:
                             v.tensor_copy(
                                 cbg_st[:, ts:ts + 1],
                                 cbg_ps[0:64, e:e + 1]),
                             waits=[cbg_free], incs=[(VE, 1)])
                gu_free[gb] = toks[0]
                st["cbg_ps_free"][0] = toks[0]
                st["t_cbg"] = toks[0]
                if ts % 8 == 7:
                    toks = sc.op("vector", lambda v: v.tensor_copy(
                        pt_st[:], pt_ps[:]),
                        waits=[st["pt_spill"]], incs=[(VE, 1)])
                    st["ptev"] = toks[0]
                    t_ptev = toks[0]
                    for q in range(8):
                        sc.op("sync", lambda s, e=e, ts=ts, q=q:
                              s.dma_start(
                                  out=pt_hbm[ts - 7 + q]
                                  [:, e * 128:(e + 1) * 128],
                                  in_=pt_st[:, q * 128:(q + 1) * 128]),
                              waits=[t_ptev] if q == 0 else (),
                              incs=[("pts", 16)])
                    st["pt_spill"] = sc.tok("pts")
                st["t_fm"] = sc.tok(VE)
                return st

            pst = {"ptev": None, "pt_spill": None, "t_fm": None,
                   "t_cbg": None, "t_x": [None, None], "t_p": [None, None],
                   "cbg_ps_free": [None, None]}
            if p3:
                wgu_tok[0][0] = load_wgu1(0, 0, 0, 0, True)
                wgu_tok[1][0] = load_wgu1(0, 0, 1, 0, True)
                wd_tok = load_wd(0, True)
                t_cbf_all = t_attn["ve"]
                for ts0 in range(NQB):
                    prep_load(0, ts0, pst, first=(ts0 == 0))
                    prep_mm(0, ts0, pst)

            for e in range(ne if p3 else 0):
                fmbuf = e % 2
                t_fm, t_cbg = pst["t_fm"], pst["t_cbg"]
                t_c1 = sc.op("sync", lambda s: s.dma_start(
                    out=cbg128[0:64, :],
                    in_=cbg_st[:].rearrange("p (j two) -> p j two", two=2)
                    [:, :, 0:1], allow_slow_non_contiguous=True),
                    waits=[t_cbg, c128_free], incs=[("cbs", 16)])[0]
                t_c2 = sc.op("sync", lambda s: s.dma_start(
                    out=cbg128[64:128, :],
                    in_=cbg_st[:].rearrange("p (j two) -> p j two", two=2)
                    [:, :, 1:2], allow_slow_non_contiguous=True),
                    incs=[("cbs", 16)])[0]
                for ib in range(NIC):
                    gib = e * NIC + ib
                    buf = gib % 2
                    nxt = (gib + 1) % 2
                    tg = wgu_tok[0][buf]
                    tu = wgu_tok[1][buf]
                    for tB in range(CAP // 512):
                        pp = (gib * (CAP // 512) + tB) % 2
                        gpsA, gpsB = gu_ps[2 * pp], gu_ps[2 * pp + 1]
                        for dc in range(NDC):
                            sc.op("tensor", lambda t, dc=dc, tB=tB, buf=buf,
                                  gpsA=gpsA, fmbuf=fmbuf: t.matmul(
                                      gpsA[:],
                                      wgu_sb[0][buf][:, dc * 128:(dc + 1) * 128],
                                      xng_fm[fmbuf][:, dc * CAP + tB * 512:
                                                    dc * CAP + tB * 512 + 512],
                                      start=(dc == 0), stop=(dc == NDC - 1)),
                                  waits=[tg, t_fm, gu_free[2 * pp]]
                                  if dc == 0 else (),
                                  incs=[(PE, 1)] if dc == NDC - 1 else ())
                        for dc in range(NDC):
                            sc.op("tensor", lambda t, dc=dc, tB=tB, buf=buf,
                                  gpsB=gpsB, fmbuf=fmbuf: t.matmul(
                                      gpsB[:],
                                      wgu_sb[1][buf][:, dc * 128:(dc + 1) * 128],
                                      xng_fm[fmbuf][:, dc * CAP + tB * 512:
                                                    dc * CAP + tB * 512 + 512],
                                      start=(dc == 0), stop=(dc == NDC - 1)),
                                  waits=[tu, gu_free[2 * pp + 1]]
                                  if dc == 0 else (),
                                  incs=[(PE, 1)] if dc == NDC - 1 else ())
                        t_gu = sc.tok(PE)
                        sc.op("scalar", lambda s, gpsA=gpsA: s.activation(
                            sg_sb[:], gpsA[:], AF.Silu),
                            waits=[t_gu, sg_free], incs=[(SE, 1)])
                        t_sg = sc.tok(SE)
                        toks = sc.op("vector", lambda v, ib=ib, tB=tB, gpsB=gpsB:
                                     v.tensor_tensor(
                                         h_sb[:, ib * CAP + tB * 512:
                                              ib * CAP + tB * 512 + 512],
                                         sg_sb[:], gpsB[:], ALU.mult),
                                     waits=[t_sg, h_free] if tB == 0 and
                                     ib == 0 else [t_sg],
                                     incs=[(VE, 1)])
                        sg_free = toks[0]
                        gu_free[2 * pp] = toks[0]
                        gu_free[2 * pp + 1] = toks[0]
                    if ib + 1 < NIC:
                        wgu_tok[0][nxt] = load_wgu1(e, ib + 1, 0, nxt, False)
                        wgu_tok[1][nxt] = load_wgu1(e, ib + 1, 1, nxt, False)
                    elif e + 1 < ne:
                        wgu_tok[0][nxt] = load_wgu1(e + 1, 0, 0, nxt, False)
                        wgu_tok[1][nxt] = load_wgu1(e + 1, 0, 1, nxt, False)
                    wgu_free[0][buf] = sc.tok(PE)
                    wgu_free[1][buf] = sc.tok(PE)
                t_h_done = sc.tok(VE)
                if debug and e == 0:
                    sc.op("sync", lambda s: s.dma_start(
                        out=dbg_xng[:, :], in_=xng_fm[0][:]),
                        waits=[t_h_done], incs=[("dbg", 16)])
                    sc.op("sync", lambda s: s.dma_start(
                        out=dbg_h2[:, :], in_=h_sb[:]),
                        incs=[("dbg", 16)])

                if e + 1 < ne:
                    wd_free = t_down_last_prev
                    prep_load(e + 1, 0, pst)
                    prep_load(e + 1, 1, pst)
                t_down_last = None
                for tsc in range(NCC):
                    for ob in range(4):
                        pb = (tsc * 4 + ob) % 2
                        for ic in range(NIC):
                            sc.op("tensor", lambda t, tsc=tsc, ob=ob, ic=ic,
                                  pb=pb: t.matmul(
                                      y_ps[pb][:],
                                      h_sb[:, ic * CAP + tsc * 128:
                                           ic * CAP + tsc * 128 + 128],
                                      wd_sb[:, ic * D + ob * 512:
                                            ic * D + ob * 512 + 512],
                                      start=(ic == 0), stop=(ic == NIC - 1)),
                                  waits=[wd_tok, t_h_done, y_free[pb]]
                                  if ic == 0 else (),
                                  incs=[(PE, 1)] if ic == NIC - 1 else ())
                        t_y = sc.tok(PE)
                        toks = sc.op("vector", lambda v, tsc=tsc, ob=ob,
                                     pb=pb: v.tensor_scalar(
                                         y_st[tsc % 2][:, ob * 512:(ob + 1) * 512],
                                         y_ps[pb][:],
                                         cbg128[:, tsc:tsc + 1],
                                         None, ALU.mult),
                                     waits=[t_y, t_c2] +
                                     ([yst_free[tsc % 2]] if ob == 0 else []),
                                     incs=[(VE, 1)])
                        y_free[pb] = toks[0]
                    t_yst = sc.tok(VE)
                    toks = sc.op("gpsimd", lambda g, e=e, tsc=tsc: g.dma_start(
                        out=ycmp_hbm[e][tsc * 128:(tsc + 1) * 128, :],
                        in_=y_st[tsc % 2][:]),
                        waits=[t_yst], incs=[("ysp%d" % (tsc % 2), 16)])
                    yst_free[tsc % 2] = toks[0]
                    if e + 1 < ne:
                        prep_mm(e + 1, 2 * tsc, pst)
                        prep_mm(e + 1, 2 * tsc + 1, pst)
                        if tsc + 1 < NCC:
                            prep_load(e + 1, 2 * tsc + 2, pst)
                            prep_load(e + 1, 2 * tsc + 3, pst)
                    t_down_last = sc.tok(PE)
                h_free = t_down_last
                c128_free = t_down_last
                cbg_free = t_c2
                xfm_free[fmbuf] = t_down_last
                t_down_last_prev = t_down_last
                if e + 1 < ne:
                    wd_free = t_down_last
                    wd_tok = load_wd(e + 1, False)
            t_ffn = {"pe": sc.tok(PE), "ve": sc.tok(VE), "se": sc.tok(SE)}
            t_ffn["ys"] = [sc.tok("ysp0"), sc.tok("ysp1"), sc.tok("pts")]

        with contextlib.ExitStack() as es4:
            atT = es4.enter_context(nc.sbuf_tensor("atT", [128, NH * TOK], BF16))
            wo_sb = es4.enter_context(nc.sbuf_tensor("wo_sb", [128, NDC * D],
                                                     BF16))
            yc_sb = [es4.enter_context(
                nc.sbuf_tensor("yc_sb0", [128, (E // 2) * D], BF16))]
            pt_sb = [es4.enter_context(
                nc.sbuf_tensor(f"pt_sb{i}", [128, (E // 2) * 128], BF16))
                for i in range(2)]
            x_t = es4.enter_context(nc.sbuf_tensor("x_t", [128, D], F32))
            o_sb = es4.enter_context(nc.sbuf_tensor("o_sb", [128, D], F32))
            ya_ps = [es4.enter_context(
                nc.psum_tensor(f"ya_ps{i}", [128, 2048], F32)) for i in range(2)]

            t_p3 = [t_ffn["pe"], t_ffn["ve"], t_ffn["se"]] + t_ffn["ys"]
            t_at = None
            for hh in range(NH if p4 else 0):
                t_at = sc.op("sync", lambda s, hh=hh: s.dma_start(
                    out=atT[:, hh * TOK:(hh + 1) * TOK], in_=attn_hbm[hh]),
                    waits=(t_p3 + [t_attn["sa0"], t_attn["sa1"]])
                    if hh == 0 else (),
                    incs=[("at_", 16)])[0]
            t_wo = None
            for dc in range(NDC if p4 else 0):
                t_wo = sc.op("sync", lambda g, dc=dc: g.dma_start(
                    out=wo_sb[:, dc * D:(dc + 1) * D],
                    in_=wob_hbm[dc * 128:(dc + 1) * 128, :]),
                    waits=t_p3 if dc == 0 else (),
                    incs=[("wo_", 16)])[0]

            yc_free = [None]
            pt_free4 = [None, None]
            xt_free4 = None
            o_free = None
            ya_free = [None, None]
            t_out_sto = None
            for tqc in range(NQB if p4 else 0):
                pset = tqc % 2
                lb = tqc % 2
                yb = 0
                t_pt = None
                for e in range(E):
                    half = (e % 2) * 64
                    pr = e // 2
                    t_pt = sc.op("sync", lambda s, tqc=tqc, lb=lb, e=e,
                                 half=half, pr=pr: s.dma_start(
                                     out=pt_sb[lb][half:half + 64,
                                                   pr * 128:(pr + 1) * 128],
                                     in_=pt_hbm[tqc][:, e * 128:(e + 1) * 128]),
                                 waits=[pt_free4[lb]] if e == 0 else (),
                                 incs=[("ptl", 16)])[0]
                t_yc = None
                for e in range(E):
                    half = (e % 2) * 64
                    pr = e // 2
                    t_yc = sc.op("sync", lambda s, e=e, tqc=tqc, yb=yb,
                                 half=half, pr=pr: s.dma_start(
                                     out=yc_sb[yb][half:half + 64,
                                                   pr * D:(pr + 1) * D],
                                     in_=ycmp_hbm[e][tqc * 64:(tqc + 1) * 64, :]),
                                 waits=[yc_free[yb]] if e == 0 else (),
                                 incs=[("ycl", 16)])[0]
                t_xld = sc.op("sync", lambda s, tqc=tqc: s.dma_start(
                    out=x_t[:], in_=x_own[tqc * 128:(tqc + 1) * 128, :]),
                    waits=[xt_free4], incs=[("xt4", 16)])[0]
                for hh in range(NH):
                    for ob in range(4):
                        sc.op("tensor", lambda t, hh=hh, ob=ob, tqc=tqc,
                              pset=pset: t.matmul(
                                  ya_ps[pset][:, ob * 512:(ob + 1) * 512],
                                  atT[:, hh * TOK + tqc * 128:
                                      hh * TOK + tqc * 128 + 128],
                                  wo_sb[:, hh * D + ob * 512:
                                        hh * D + ob * 512 + 512],
                                  start=(hh == 0), stop=False),
                              waits=[t_at, t_wo, ya_free[pset]]
                              if (hh == 0 and ob == 0) else (),
                              incs=())
                for pr in range(E // 2):
                    for ob in range(4):
                        last = (pr == E // 2 - 1 and ob == 3)
                        sc.op("tensor", lambda t, pr=pr, ob=ob, lb=lb, yb=yb,
                              pset=pset: t.matmul(
                                  ya_ps[pset][:, ob * 512:(ob + 1) * 512],
                                  pt_sb[lb][:, pr * 128:(pr + 1) * 128],
                                  yc_sb[yb][:, pr * D + ob * 512:
                                            pr * D + ob * 512 + 512],
                                  start=False, stop=(pr == E // 2 - 1)),
                              waits=[t_pt, t_yc] if (pr == 0 and ob == 0)
                              else (),
                              incs=[(PE, 1)] if last else ())
                t_ya = sc.tok(PE)
                pt_free4[lb] = t_ya
                yc_free[yb] = t_ya
                sc.op("vector", lambda v: v.tensor_tensor(
                    o_sb[:], x_t[:], bo_bc[:], ALU.add),
                    waits=[t_xld, o_free] + (t_p3 if tqc == 0 else []),
                    incs=[(VE, 1)])
                xt_free4 = sc.tok(VE)
                for ob in range(4):
                    sc.op("vector", lambda v, ob=ob, pset=pset: v.tensor_tensor(
                        o_sb[:, ob * 512:(ob + 1) * 512],
                        o_sb[:, ob * 512:(ob + 1) * 512],
                        ya_ps[pset][:, ob * 512:(ob + 1) * 512], ALU.add),
                        waits=[sc.tok(VE)] + ([t_ya] if ob == 0 else []),
                        incs=[(VE, 1)])
                t_osum = sc.tok(VE)
                ya_free[pset] = t_osum
                t_out_sto = sc.op("sync", lambda s, tqc=tqc: s.dma_start(
                    out=out_h[tqc * 128:(tqc + 1) * 128, :], in_=o_sb[:]),
                    waits=[t_osum], incs=[("so", 16)])[0]
                o_free = t_out_sto

            if sc.counts["so"]:
                sc.op("sync", lambda s: s.nop(), waits=[sc.tok("so")])

        sems = {}
        for name in list(sc.counts):
            sems[name] = es.enter_context(nc.semaphore(name))

        block = es.enter_context(nc.Block())

        def runner(engname):
            def run(eng):
                for engine, fn, waits, incs in sc.ops:
                    if engine != engname:
                        continue
                    for s, v in waits:
                        eng.wait_ge(sems[s], v)
                    inst = fn(eng)
                    for s, n in incs:
                        inst = inst.then_inc(sems[s], n)
            return run

        block.tensor(runner("tensor"))
        block.vector(runner("vector"))
        block.scalar(runner("scalar"))
        block.gpsimd(runner("gpsimd"))
        block.sync(runner("sync"))

    return nc


_CACHE = {}


def _get_nc():
    if "nc" not in _CACHE:
        _CACHE["nc"] = build_nc()
    return _CACHE["nc"]


def kernel(**inputs):
    x = np.asarray(inputs["L0_attn_hidden_states"], dtype=np.float32)
    wq = np.asarray(inputs["L0_attn_wq"], dtype=np.float32)
    wk = np.asarray(inputs["L0_attn_wk"], dtype=np.float32)
    wv = np.asarray(inputs["L0_attn_wv"], dtype=np.float32)
    wo = np.asarray(inputs["L0_attn_wo"], dtype=np.float32)
    shared = {
        "wqT": np.ascontiguousarray(wq.T),
        "wkT": np.ascontiguousarray(wk.T),
        "wvT": np.ascontiguousarray(wv.T),
        "woT": np.ascontiguousarray(wo.T),
        "bq": np.ascontiguousarray(inputs["L0_attn_bq"], dtype=np.float32),
        "bk": np.ascontiguousarray(inputs["L0_attn_bk"], dtype=np.float32),
        "bv": np.ascontiguousarray(inputs["L0_attn_bv"], dtype=np.float32),
        "bo": np.ascontiguousarray(inputs["L0_attn_bo"], dtype=np.float32),
        "wr": np.ascontiguousarray(inputs["L0_ffn_Wr"], dtype=np.float32),
        "wg": np.ascontiguousarray(inputs["L0_ffn_Wg"], dtype=np.float32),
        "wu": np.ascontiguousarray(inputs["L0_ffn_Wu"], dtype=np.float32),
        "wd": np.ascontiguousarray(inputs["L0_ffn_Wd"], dtype=np.float32),
    }
    nc = _get_nc()
    in_maps = []
    for c in range(8):
        b, half = divmod(c, 2)
        m = dict(shared)
        m["x_own"] = np.ascontiguousarray(x[b, half * TOK:(half + 1) * TOK])
        m["x_part"] = np.ascontiguousarray(
            x[b, (1 - half) * TOK:(2 - half) * TOK])
        in_maps.append(m)
    res = run_bass_kernel_spmd(nc, in_maps, core_ids=list(range(8)))
    out = np.empty((4, SEQ, D), np.float32)
    for c in range(8):
        b, half = divmod(c, 2)
        out[b, half * TOK:(half + 1) * TOK] = res.results[c]["out"]
    return out


if __name__ == "__main__":
    nc = build_nc()
    print("BUILD OK, instructions:", nc.n_instructions())
